# version 20
# baseline (speedup 1.0000x reference)
import numpy as np
from contextlib import ExitStack
import concourse.bass as bass
import concourse.mybir as mybir
from concourse.bass_utils import run_bass_kernel_spmd

F32 = mybir.dt.float32
BF16 = mybir.dt.bfloat16
F16 = mybir.dt.float16
I32 = mybir.dt.int32
AF = mybir.ActivationFunctionType
OP = mybir.AluOpType
AX = mybir.AxisListType

S, D, FF = 2048, 1024, 4096
NT, TT = 16, 4
NA, H = 4, 256
EPS_LN, EPS_Q = 1e-5, 1e-8
RG = [[0, 1, 2, 3, 4, 5, 6, 7]]
INV_D = 1.0 / D


class Ctr:
    def __init__(self, sem):
        self.sem = sem
        self.n = 0
        self.max_waited = 0

    def inc(self, ins, k):
        ins.then_inc(self.sem, k)
        self.n += k
        return self.n

    def wait(self, eng, val=None):
        if val is None:
            val = self.n
        if val > 0:
            eng.wait_ge(self.sem, val)
            self.max_waited = max(self.max_waited, val)


DEBUG_OUTPUTS = False


def build_nc():
    nc = bass.Bass("TRN2", target_bir_lowering=False)
    dbg = "ExternalOutput" if DEBUG_OUTPUTS else "Internal"

    x_in = nc.dram_tensor("x0", [S, D], F32, kind="ExternalInput").ap()
    qt_in = nc.dram_tensor("qwT", [NA, D, H], F32, kind="ExternalInput").ap()
    c_in = nc.dram_tensor("cb", [NA, D, H], F32, kind="ExternalInput").ap()
    w1_in = nc.dram_tensor("w1", [FF, D], F32, kind="ExternalInput").ap()
    b1_in = nc.dram_tensor("b1", [FF], F32, kind="ExternalInput").ap()
    w2_in = nc.dram_tensor("w2", [D, FF], F32, kind="ExternalInput").ap()
    b2_in = nc.dram_tensor("b2", [D], F32, kind="ExternalInput").ap()
    y_out = nc.dram_tensor("y", [S, D], F32, kind="ExternalOutput").ap()

    ln1_d = nc.dram_tensor("ln1_d", [S, D], BF16, kind=dbg).ap()
    x1_d = nc.dram_tensor("x1_d", [S, D], F32, kind=dbg).ap()
    k1_d = nc.dram_tensor("k1_d", [S, D], BF16, kind=dbg).ap()
    ao_d = nc.dram_tensor("ao_d", [D, S], BF16, kind=dbg).ap()
    o2_d = nc.dram_tensor("o2_d", [D, S], BF16, kind=dbg).ap()
    j1_d = nc.dram_tensor("j1_d", [FF, D], BF16, kind=dbg).ap()
    j2_d = nc.dram_tensor("j2_d", [D, FF], BF16, kind=dbg).ap()
    h_d = nc.dram_tensor("h_d", [FF, S], F16, kind=dbg).ap()
    dbg_d = nc.dram_tensor("dbg_d", [128, 256], F32, kind=dbg).ap()
    dbg2_d = nc.dram_tensor("dbg2_d", [128, 256], F32, kind=dbg).ap()
    dbg3_d = nc.dram_tensor("dbg3_d", [128, 4096], BF16, kind=dbg).ap()
    cc1_i = nc.dram_tensor("cc1_i", [1, 16], F32, kind="Internal").ap()
    cc1_o = nc.dram_tensor("cc1_o", [1, 16], F32, kind="Internal").ap()
    cc2_i = nc.dram_tensor("cc2_i", [1, 16], F32, kind="Internal").ap()
    cc2_o = nc.dram_tensor("cc2_o", [1, 16], F32, kind="Internal").ap()

    top = ExitStack()
    with top:
        def sbuf(name, shape, dt):
            return top.enter_context(nc.sbuf_tensor(name, shape, dt))

        stm = sbuf("stm", [128, 64], F32)
        srr = sbuf("srr", [128, 64], F32)
        sA = sbuf("sA", [128, 64], F32)
        sB = sbuf("sB", [128, 64], F32)
        sC = sbuf("sC", [128, 64], F32)
        sDs = sbuf("sDs", [128, 64], F32)
        ssum = sbuf("ssum", [128, 64], F32)
        ssq = sbuf("ssq", [128, 64], F32)
        rx = sbuf("rx", [128, 16], F32)
        hm = sbuf("hm", [128, 128], F32)
        sw1m = sbuf("sw1m", [128, 32], F32)
        sb1 = sbuf("sb1", [128, 32], F32)
        eb1 = sbuf("eb1", [128, 32], F32)
        bq1 = sbuf("bq1", [128, 32], F32)
        sw2m = sbuf("sw2m", [128, 8], F32)
        sb2 = sbuf("sb2", [128, 8], F32)
        eb2 = sbuf("eb2", [128, 8], F32)
        bq2 = sbuf("bq2", [128, 8], F32)
        b1c = sbuf("b1c", [128, 32], F32)
        b2c = sbuf("b2c", [128, 8], F32)
        ident = sbuf("ident", [128, 128], F32)
        io1 = sbuf("io1", [128, 128], F32)
        io2 = sbuf("io2", [128, 1], F32)
        ones = sbuf("ones", [1, 128], F32)
        gb = sbuf("gb", [128, 8], F32)
        gsc = sbuf("gsc", [1, 16], F32)
        scr = sbuf("scr", [128, 2048], F32)
        sci = sbuf("sci", [128, 2048], I32)
        tny = sbuf("tny", [128, 8], F32)

        C = {}
        for name in ["w1s0", "w1s1", "w1s2", "w1s3", "w1s4", "w1s5", "w1s6", "w1s7",
                     "w2s0", "w2s1", "x1p0", "x1p1", "ldhp0", "ldhp1",
                     "ldx", "ldq", "stln", "trln",
                     "stao", "trao", "stx1", "stk1", "trk1",
                     "stj1", "trj1", "stj2", "trj2", "sth",
                     "sto2", "tro2", "stout", "cc", "ccd",
                     "V", "A", "T", "G"]:
            C[name] = Ctr(top.enter_context(nc.semaphore(name)))
        V, A, T, G = C["V"], C["A"], C["T"], C["G"]

        blk = top.enter_context(nc.Block())

        ph_x = ExitStack()
        xres = ph_x.enter_context(nc.sbuf_tensor("xres", [128, NT * D], F32))
        sqs = ph_x.enter_context(nc.sbuf_tensor("sqs", [128, 2048], F32))
        p01 = ExitStack()
        pb = p01.enter_context(nc.psum_tensor("pb", [128, 512], F32))
        pm = [p01.enter_context(nc.psum_tensor("pm%d" % q, [128, 512], F32))
              for q in range(NA)]

        @blk.sync
        def _(sy):
            for hh in range(2):
                C["ldx"].inc(sy.dma_start(
                    out=xres[:, hh * 8 * D:(hh + 1) * 8 * D],
                    in_=x_in[hh * 1024:(hh + 1) * 1024, :].rearrange(
                        "(t p) d -> p t d", p=128)), 16)

        attn = ExitStack()
        qtb = attn.enter_context(nc.sbuf_tensor("qtb", [128, 32 * H], BF16))
        wpp = attn.enter_context(nc.sbuf_tensor("wpp", [128, 16 * 512], BF16))
        cbf = attn.enter_context(nc.sbuf_tensor("cbf", [128, 32 * H], BF16))
        lnr = attn.enter_context(nc.sbuf_tensor("lnr", [128, 2 * D], BF16))
        Mbf = attn.enter_context(nc.sbuf_tensor("Mbf", [128, 8 * H], BF16))
        ln1T = attn.enter_context(nc.sbuf_tensor("ln1T", [128, 8 * 512], BF16))
        atb = attn.enter_context(nc.sbuf_tensor("atb", [128, 2 * 8 * 512], BF16))
        aob = attn.enter_context(nc.sbuf_tensor("aob", [128, 2 * 512], BF16))
        aotm = attn.enter_context(nc.sbuf_tensor("aotm", [128, 4 * D], BF16))
        w1r = attn.enter_context(nc.sbuf_tensor("w1r", [128, 4 * D], F32))
        j1r = attn.enter_context(nc.sbuf_tensor("j1r", [128, 2 * D], BF16))

        @blk.gpsimd
        def _(gp):
            gp.iota(io1[:], pattern=[[1, 128]], base=0, channel_multiplier=0,
                    allow_small_or_imprecise_dtypes=True)
            G.inc(gp.iota(io2[:], pattern=[[0, 1]], base=0, channel_multiplier=1,
                          allow_small_or_imprecise_dtypes=True), 1)
            for a in range(NA):
                for r in range(8):
                    C["ldq"].inc(gp.dma_start(
                        out=qtb[:, (a * 8 + r) * H:(a * 8 + r + 1) * H],
                        in_=qt_in[a, r * 128:(r + 1) * 128, :]), 16)
            for a in range(NA):
                for kc in range(8):
                    C["ldq"].inc(gp.dma_start(
                        out=cbf[:, (a * 8 + kc) * H:(a * 8 + kc + 1) * H],
                        in_=c_in[a, kc * 128:(kc + 1) * 128, :]), 16)
            C["ldq"].inc(gp.dma_start(
                out=scr[0:32, 0:128],
                in_=b1_in.rearrange("(a b) -> a b", b=128)), 16)
            C["ldq"].inc(gp.dma_start(
                out=scr[0:8, 128:256],
                in_=b2_in.rearrange("(a b) -> a b", b=128)), 16)

        @blk.vector
        def _(ve):
            G.wait(ve)
            ve.tensor_scalar(out=ident[:], in0=io1[:], scalar1=io2[:, 0:1],
                             scalar2=None, op0=OP.is_equal)
            ve.memset(ones[:], 1.0)
            ve.drain()
            V.inc(ve.memset(gsc[:], 0.0), 1)
        v_ident = V.n

        @blk.tensor
        def _(te):
            C["ldq"].wait(te)
            V.wait(te, v_ident)
            te.transpose(pb[:, 0:32], scr[0:32, 0:128], ident[0:32, 0:32])
            T.inc(te.transpose(pb[:, 32:40], scr[0:8, 128:256], ident[0:8, 0:8]), 1)
        t_btr = T.n

        @blk.vector
        def _(ve):
            T.wait(ve, t_btr)
            ve.tensor_copy(b1c[:], pb[:, 0:32])
            ve.drain()
            V.inc(ve.tensor_copy(b2c[:], pb[:, 32:40]), 1)
        v_bc = V.n

        @blk.vector
        def _(ve):
            for ti in range(NT):
                if ti % 8 == 0:
                    C["ldx"].wait(ve, 16 * (ti // 8 + 1))
                xt_ = xres[:, ti * D:(ti + 1) * D]
                ve.tensor_tensor(out=sqs[:, 0:1024], in0=xt_, in1=xt_, op=OP.mult)
                ve.reduce_sum(ssum[:, ti:ti + 1], xt_, axis=AX.X)
                ins = ve.reduce_sum(ssq[:, ti:ti + 1], sqs[:, 0:1024], axis=AX.X)
            V.inc(ins, 1)
        v_ln1sum = V.n
        a_ln1sq = A.n

        def ln_finalize(lo, n):
            h = n // 2
            sla, slb = slice(lo, lo + h), slice(lo + h, lo + n)
            sl = slice(lo, lo + n)

            @blk.vector
            def _(ve):
                ve.drain()
                ve.tensor_scalar(out=stm[:, sla], in0=ssum[:, sla], scalar1=INV_D,
                                 scalar2=None, op0=OP.mult)
                ve.tensor_scalar(out=stm[:, slb], in0=ssum[:, slb], scalar1=INV_D,
                                 scalar2=None, op0=OP.mult)
                ve.tensor_scalar(out=sA[:, sla], in0=ssq[:, sla], scalar1=INV_D,
                                 scalar2=EPS_LN, op0=OP.mult, op1=OP.add)
                ve.tensor_scalar(out=sA[:, slb], in0=ssq[:, slb], scalar1=INV_D,
                                 scalar2=EPS_LN, op0=OP.mult, op1=OP.add)
                ve.tensor_tensor(out=sB[:, sla], in0=stm[:, sla], in1=stm[:, sla],
                                 op=OP.mult)
                ve.tensor_tensor(out=sB[:, slb], in0=stm[:, slb], in1=stm[:, slb],
                                 op=OP.mult)
                ve.tensor_tensor(out=sDs[:, sla], in0=sA[:, sla], in1=sB[:, sla],
                                 op=OP.subtract)
                V.inc(ve.tensor_tensor(out=sDs[:, slb], in0=sA[:, slb],
                                       in1=sB[:, slb], op=OP.subtract), 1)
            v1 = V.n

            @blk.scalar
            def _(sc):
                V.wait(sc, v1)
                A.inc(sc.activation(out=sB[:, sl], in_=sDs[:, sl], func=AF.Sqrt), 1)
            a1 = A.n

            @blk.vector
            def _(ve):
                A.wait(ve, a1)
                ve.drain()
                ve.reciprocal(sC[:, sla], sB[:, sla])
                ve.reciprocal(sC[:, slb], sB[:, slb])
                ve.tensor_tensor(out=sA[:, sla], in0=sC[:, sla], in1=sC[:, sla],
                                 op=OP.mult)
                ve.tensor_tensor(out=sA[:, slb], in0=sC[:, slb], in1=sC[:, slb],
                                 op=OP.mult)
                ve.tensor_tensor(out=sB[:, sla], in0=sDs[:, sla], in1=sA[:, sla],
                                 op=OP.mult)
                ve.tensor_tensor(out=sB[:, slb], in0=sDs[:, slb], in1=sA[:, slb],
                                 op=OP.mult)
                ve.tensor_scalar(out=sA[:, sla], in0=sB[:, sla], scalar1=-0.5,
                                 scalar2=1.5, op0=OP.mult, op1=OP.add)
                ve.tensor_scalar(out=sA[:, slb], in0=sB[:, slb], scalar1=-0.5,
                                 scalar2=1.5, op0=OP.mult, op1=OP.add)
                ve.tensor_tensor(out=srr[:, sla], in0=sC[:, sla], in1=sA[:, sla],
                                 op=OP.mult)
                V.inc(ve.tensor_tensor(out=srr[:, slb], in0=sC[:, slb],
                                       in1=sA[:, slb], op=OP.mult), 1)
            return V.n

        v_ln1finA = ln_finalize(0, 8)
        v_ln1fin = ln_finalize(8, 8)

        if DEBUG_OUTPUTS:
            @blk.gpsimd
            def _(gp):
                V.wait(gp, v_ln1fin)
                C["ccd"].inc(gp.dma_start(out=dbg2_d[:, 0:64], in_=srr[:]), 16)
                C["ccd"].inc(gp.dma_start(out=dbg2_d[:, 64:128], in_=sC[:]), 16)
                C["ccd"].inc(gp.dma_start(out=dbg2_d[:, 192:256], in_=sDs[:]), 16)
                C["ccd"].wait(gp)

        v_ln1ap = []
        t_gram = []
        for ti in range(NT):
            slot = ti % 2

            @blk.vector
            def _(ve, ti=ti, slot=slot):
                if ti == 0:
                    V.wait(ve, v_ln1finA)
                    ve.drain()
                if ti == 8:
                    V.wait(ve, v_ln1fin)
                    ve.drain()
                if ti >= 2:
                    T.wait(ve, t_gram[ti - 2])
                    C["stln"].wait(ve, 16 * (ti - 1))
                V.inc(ve.tensor_scalar(out=lnr[:, slot * D:(slot + 1) * D],
                                       in0=xres[:, ti * D:(ti + 1) * D],
                                       scalar1=stm[:, ti:ti + 1],
                                       scalar2=srr[:, ti:ti + 1],
                                       op0=OP.subtract, op1=OP.mult), 1)
            v_ln1ap.append(V.n)

            @blk.tensor
            def _(te, ti=ti, slot=slot):
                V.wait(te, v_ln1ap[ti])
                for q in range(NA):
                    base = slot * D + q * H
                    for fc in range(2):
                        ins = te.matmul(pm[q][:, fc * H:(fc + 1) * H],
                                        lnr[:, base + fc * 128:base + fc * 128 + 128],
                                        lnr[:, base:base + H],
                                        start=(ti == 0 and fc == 0),
                                        stop=(ti == NT - 1))
                T.inc(ins, 1)
            t_gram.append(T.n)

            @blk.gpsimd
            def _(gp, ti=ti, slot=slot):
                V.wait(gp, v_ln1ap[ti])
                C["stln"].inc(gp.dma_start(out=ln1_d[ti * 128:(ti + 1) * 128, :],
                                           in_=lnr[:, slot * D:(slot + 1) * D]), 16)

        @blk.vector
        def _(ve):
            T.wait(ve, t_gram[-1])
            for q in range(NA):
                for fc in range(2):
                    ins = ve.tensor_copy(Mbf[:, (q * 2 + fc) * H:(q * 2 + fc + 1) * H],
                                         pm[q][:, fc * H:(fc + 1) * H])
            V.inc(ins, 1)
        v_mevac = V.n

        if DEBUG_OUTPUTS:
            @blk.gpsimd
            def _(gp):
                V.wait(gp, v_mevac)
                C["ccd"].inc(gp.dma_start(out=dbg3_d[:, 0:2048], in_=Mbf[:]), 16)
                C["ccd"].inc(gp.dma_start(out=dbg2_d[:, 128:192], in_=srr[:]), 16)
                C["ccd"].wait(gp)

        p01.close()
        p2 = ExitStack()
        pat = p2.enter_context(nc.psum_tensor("pat", [128, 2048], F32))
        po = p2.enter_context(nc.psum_tensor("po", [128, 2048], F32))

        t_wpp = []
        a_wpp = []
        for i in range(16):
            a, q = i // 4, i % 4
            boff = (i % 2) * 512

            @blk.tensor
            def _(te, a=a, q=q, boff=boff, i=i):
                if i == 0:
                    V.wait(te, v_mevac)
                    C["ldq"].wait(te)
                if i >= 2:
                    A.wait(te, a_wpp[i - 2])
                for ec in range(2):
                    for fc in range(2):
                        ins = te.matmul(
                            pat[:, boff + ec * H:boff + ec * H + H],
                            qtb[:, (a * 8 + q * 2 + fc) * H + ec * 128:
                                (a * 8 + q * 2 + fc) * H + ec * 128 + 128],
                            Mbf[:, (q * 2 + fc) * H:(q * 2 + fc + 1) * H],
                            start=(ec == 0 and fc == 0), stop=(ec == 1 and fc == 1))
                T.inc(ins, 1)
            t_wpp.append(T.n)

            @blk.scalar
            def _(sc, i=i, boff=boff):
                T.wait(sc, t_wpp[i])
                A.inc(sc.copy(out=wpp[:, i * 512:(i + 1) * 512],
                              in_=pat[:, boff:boff + 512]), 1)
            a_wpp.append(A.n)

        t_at = {}
        a_at = {}
        a_at_par = {}
        t_op = {}
        t_op_par = {}
        v_ao = {}
        v_ao_par = {}
        t_lastAT = [0] * TT
        v_w1q = [0] * 32
        v_ln2ap = {}

        def emit_AT(tt, a, q):
            @blk.tensor
            def _(te):
                if a == 0 and q == 0:
                    C["trln"].wait(te, 16 * 8 * (tt + 1))
                    if tt == 0:
                        A.wait(te, a_wpp[-1])
                pk = q % 2
                if pk in a_at_par:
                    eng, cnt = a_at_par[pk]
                    C[eng].wait(te, cnt)
                for gc in range(2):
                    for ec in range(2):
                        ins = te.matmul(
                            pat[:, pk * 1024 + gc * 512:pk * 1024 + gc * 512 + 512],
                            wpp[:, (a * 4 + q) * 512 + ec * H + gc * 128:
                                (a * 4 + q) * 512 + ec * H + gc * 128 + 128],
                            ln1T[:, (a * 2 + ec) * 512:(a * 2 + ec + 1) * 512],
                            start=(ec == 0), stop=(ec == 1))
                T.inc(ins, 1)
            t_at[(tt, a, q)] = T.n
            if a == NA - 1 and q == NA - 1:
                t_lastAT[tt] = T.n

            @blk.scalar
            def _(sc):
                T.wait(sc, t_at[(tt, a, q)])
                ap = a % 2
                if ap in t_op_par:
                    T.wait(sc, t_op_par[ap])
                pk = q % 2
                for gc in range(2):
                    ins = sc.copy(
                        out=atb[:, ap * 4096 + (q * 2 + gc) * 512:
                                ap * 4096 + (q * 2 + gc + 1) * 512],
                        in_=pat[:, pk * 1024 + gc * 512:pk * 1024 + gc * 512 + 512])
                A.inc(ins, 1)
            a_at[(tt, a, q)] = ("A", A.n)
            a_at_par[q % 2] = a_at[(tt, a, q)]

        def emit_OP(tt, a):
            @blk.tensor
            def _(te):
                A.wait(te, a_at[(tt, a, 3)][1])
                ap = a % 2
                if ap in v_ao_par:
                    V.wait(te, v_ao_par[ap])
                for oc in range(2):
                    for kc in range(8):
                        ins = te.matmul(
                            po[:, ap * 1024 + oc * 512:ap * 1024 + oc * 512 + 512],
                            cbf[:, (a * 8 + kc) * H + oc * 128:
                                (a * 8 + kc) * H + oc * 128 + 128],
                            atb[:, ap * 4096 + kc * 512:ap * 4096 + (kc + 1) * 512],
                            start=(kc == 0), stop=(kc == 7))
                T.inc(ins, 1)
            t_op[(tt, a)] = T.n
            t_op_par[a % 2] = T.n

            @blk.vector
            def _(ve):
                T.wait(ve, t_op[(tt, a)])
                n_prev = len(v_ao)
                if n_prev >= 1:
                    C["stao"].wait(ve, 16 * 2 * n_prev)
                ap = a % 2
                for oc in range(2):
                    ins = ve.tensor_copy(aob[:, oc * 512:(oc + 1) * 512],
                                         po[:, ap * 1024 + oc * 512:
                                             ap * 1024 + oc * 512 + 512])
                V.inc(ins, 1)
            v_ao[(tt, a)] = V.n
            v_ao_par[a % 2] = V.n

            @blk.gpsimd
            def _(gp):
                V.wait(gp, v_ao[(tt, a)])
                for oc in range(2):
                    C["stao"].inc(gp.dma_start(
                        out=ao_d[a * H + oc * 128:a * H + oc * 128 + 128,
                                 tt * 512:(tt + 1) * 512],
                        in_=aob[:, oc * 512:(oc + 1) * 512]), 16)

        for tt in range(TT):
            @blk.sync
            def _(sy, tt=tt):
                C["stln"].wait(sy, 16 * NT)
                if tt >= 1:
                    T.wait(sy, t_lastAT[tt - 1])
                for ec in range(8):
                    C["trln"].inc(sy.dma_start_transpose(
                        ln1T[:, ec * 512:(ec + 1) * 512],
                        ln1_d[tt * 512:(tt + 1) * 512, ec * 128:(ec + 1) * 128]), 16)
                for fc in range(tt * 8, tt * 8 + 4):
                    if fc >= 4:
                        V.wait(sy, v_w1q[fc - 4])
                    C["w1s%d" % (fc % 4)].inc(sy.dma_start(
                        out=w1r[:, (fc % 4) * D:(fc % 4 + 1) * D],
                        in_=w1_in[fc * 128:(fc + 1) * 128, :]), 16)

            for a in range(NA):
                for q in range(NA):
                    emit_AT(tt, a, q)
                if a >= 1:
                    emit_OP(tt, a - 1)
            emit_OP(tt, NA - 1)

            @blk.scalar
            def _(sc, tt=tt):
                C["stao"].wait(sc, 16 * 8 * (tt + 1))
                if tt > 0:
                    V.wait(sc, v_ln2ap[tt * 4 - 1])
                    A.wait(sc, A.n)
                for tc in range(4):
                    ti = tt * 4 + tc
                    C["trao"].inc(sc.dma_start_transpose(
                        aotm[:, tc * D:(tc + 1) * D],
                        ao_d[0:D, ti * 128:(ti + 1) * 128]), 16)

            @blk.vector
            def _(ve, tt=tt):
                C["trao"].wait(ve, 16 * 4 * (tt + 1))
                for tc in range(4):
                    ti = tt * 4 + tc
                    ins = ve.reduce_sum(ssum[:, 16 + ti:17 + ti],
                                        aotm[:, tc * D:(tc + 1) * D], axis=AX.X)
                V.inc(ins, 1)
            v_ln2sum = V.n

            @blk.vector
            def _(ve, tt=tt):
                C["trao"].wait(ve, 16 * 4 * (tt + 1))
                for tc in range(4):
                    ti = tt * 4 + tc
                    src_ = aotm[:, tc * D:(tc + 1) * D]
                    ve.tensor_tensor(out=sqs[:, 0:1024], in0=src_, in1=src_,
                                     op=OP.mult)
                    ve.reduce_sum(ssum[:, 16 + ti:17 + ti], src_, axis=AX.X)
                    ins = ve.reduce_sum(ssq[:, 16 + ti:17 + ti], sqs[:, 0:1024],
                                        axis=AX.X)
                V.inc(ins, 1)
            v_sq2 = [V.n]

            v_ln2fin = ln_finalize(16 + tt * 4, 4)

            for tc in range(4):
                ti = tt * 4 + tc

                @blk.vector
                def _(ve, tt=tt, tc=tc, ti=ti):
                    if tc == 0:
                        V.wait(ve, v_ln2fin)
                        ve.drain()
                    xt_ = xres[:, ti * D:(ti + 1) * D]
                    ve.tensor_scalar(out=scr[:, 1024:2048],
                                     in0=aotm[:, tc * D:(tc + 1) * D],
                                     scalar1=stm[:, 16 + ti:17 + ti],
                                     scalar2=srr[:, 16 + ti:17 + ti],
                                     op0=OP.subtract, op1=OP.mult)
                    ve.tensor_tensor(out=xt_, in0=xt_, in1=scr[:, 1024:2048],
                                     op=OP.add)
                    ve.tensor_tensor(out=sqs[:, 0:1024], in0=xt_, in1=xt_,
                                     op=OP.mult)
                    ve.tensor_copy(ssum[:, 32 + ti:33 + ti], ssum[:, ti:ti + 1])
                    ve.reduce_max(rx[:, ti:ti + 1], xt_, axis=AX.X)
                    V.inc(ve.reduce_sum(ssq[:, 32 + ti:33 + ti], sqs[:, 0:1024],
                                        axis=AX.X), 1)
                v_ln2ap[ti] = V.n

                @blk.gpsimd
                def _(gp, ti=ti):
                    V.wait(gp, v_ln2ap[ti])
                    C["stx1"].inc(gp.dma_start(out=x1_d[ti * 128:(ti + 1) * 128, :],
                                               in_=xres[:, ti * D:(ti + 1) * D]), 16)

            def emit_w1q(fc0, n):
                @blk.vector
                def _(ve):
                    if fc0 == 0:
                        T.wait(ve, t_btr)
                    for i, fc in enumerate(range(fc0, fc0 + n)):
                        C["w1s%d" % (fc % 4)].wait(ve, 16 * (fc // 4 + 1))
                        ve.reduce_max(tny[:, i:i + 1],
                                      w1r[:, (fc % 4) * D:(fc % 4 + 1) * D],
                                      axis=AX.X, apply_absolute_value=True)
                    ve.drain()
                    ve.tensor_scalar(out=sw1m[:, fc0:fc0 + n], in0=tny[:, 0:n],
                                     scalar1=EPS_Q, scalar2=None, op0=OP.max)
                    ve.drain()
                    ve.reciprocal(tny[:, 4:4 + n], sw1m[:, fc0:fc0 + n])
                    ve.drain()
                    ve.tensor_scalar(out=tny[:, 0:n], in0=tny[:, 4:4 + n],
                                     scalar1=127.0, scalar2=None, op0=OP.mult)
                    ve.drain()
                    for i, fc in enumerate(range(fc0, fc0 + n)):
                        if fc >= 2:
                            C["stj1"].wait(ve, 16 * fc)
                        ve.tensor_scalar(out=sci[:, (i % 2) * 1024:(i % 2 + 1) * 1024],
                                         in0=w1r[:, (fc % 4) * D:(fc % 4 + 1) * D],
                                         scalar1=tny[:, i:i + 1], scalar2=None,
                                         op0=OP.mult)
                        V.inc(ve.tensor_copy(j1r[:, (fc % 2) * D:(fc % 2 + 1) * D],
                                             sci[:, (i % 2) * 1024:(i % 2 + 1) * 1024]), 1)
                        v_w1q[fc] = V.n

                @blk.gpsimd
                def _(gp):
                    for fc in range(fc0, fc0 + n):
                        V.wait(gp, v_w1q[fc])
                        C["stj1"].inc(gp.dma_start(
                            out=j1_d[fc * 128:(fc + 1) * 128, :],
                            in_=j1r[:, (fc % 2) * D:(fc % 2 + 1) * D]), 16)

            emit_w1q(tt * 8, 4)

            @blk.sync
            def _(sy, tt=tt):
                for fc in range(tt * 8 + 4, tt * 8 + 8):
                    V.wait(sy, v_w1q[fc - 4])
                    C["w1s%d" % (fc % 4)].inc(sy.dma_start(
                        out=w1r[:, (fc % 4) * D:(fc % 4 + 1) * D],
                        in_=w1_in[fc * 128:(fc + 1) * 128, :]), 16)

            emit_w1q(tt * 8 + 4, 4)

        a_ln3sq = A.n

        @blk.vector
        def _(ve):
            A.wait(ve, a_ln3sq)
        v_ln3fin = ln_finalize(32, 16)

        attn.close()
        p2.close()
        ph_x.close()

        phF = ExitStack()
        pbc = phF.enter_context(nc.psum_tensor("pbc", [128, 16], F32))
        pcc = phF.enter_context(nc.psum_tensor("pcc", [1, 128], F32))
        x1rd = phF.enter_context(nc.sbuf_tensor("x1rd", [128, 2 * D], F32))

        @blk.vector
        def _(ve):
            V.wait(ve, v_ln3fin)
            ve.drain()
            ve.tensor_tensor(out=sA[:, 0:16], in0=rx[:, 0:16], in1=stm[:, 32:48],
                             op=OP.subtract)
            ve.drain()
            ve.tensor_tensor(out=sB[:, 0:16], in0=sA[:, 0:16], in1=srr[:, 32:48],
                             op=OP.mult)
            ve.drain()
            V.inc(ve.reduce_max(tny[:, 3:4], sB[:, 0:16], axis=AX.X), 1)
        v_lmax1 = V.n

        @blk.tensor
        def _(te):
            V.wait(te, v_lmax1)
            T.inc(te.transpose(pcc[0:1, :], tny[:, 3:4], ident[:]), 1)
        t_max1 = T.n

        @blk.vector
        def _(ve):
            T.wait(ve, t_max1)
            ve.reduce_max(gsc[0:1, 1:2], pcc[0:1, :], axis=AX.X)
            ve.drain()
            V.inc(ve.tensor_scalar(out=gsc[0:1, 0:1], in0=gsc[0:1, 1:2],
                                   scalar1=EPS_Q, scalar2=None, op0=OP.max), 1)
        v_g1 = V.n

        @blk.gpsimd
        def _(gp):
            V.wait(gp, v_g1)
            C["stx1"].wait(gp, 16 * NT)
            C["stj1"].wait(gp, 16 * 32)
            C["ccd"].inc(gp.dma_start(out=cc1_i[:], in_=gsc[:]), 16)
            C["ccd"].wait(gp)
            C["cc"].inc(gp.collective_compute(
                "AllReduce", OP.max, replica_groups=RG,
                ins=[cc1_i[:]], outs=[cc1_o[:]]), 1)
            C["cc"].wait(gp)
            C["ccd"].inc(gp.dma_start(out=gsc[0:1, 2:3], in_=cc1_o[0:1, 0:1]), 16)
            G.inc(gp.nop(), 1)
        g_cc1 = G.n

        @blk.tensor
        def _(te):
            C["ccd"].wait(te)
            G.wait(te, g_cc1)
            T.inc(te.matmul(pbc[:, 0:1], ones[:], gsc[0:1, 2:3],
                            start=True, stop=True), 1)
        t_bc1 = T.n

        @blk.vector
        def _(ve):
            T.wait(ve, t_bc1)
            V.wait(ve, v_bc)
            ve.tensor_copy(gb[:, 0:1], pbc[:, 0:1])
            ve.drain()
            ve.reciprocal(gb[:, 1:2], gb[:, 0:1])
            ve.drain()
            ve.tensor_scalar(out=gb[:, 2:3], in0=gb[:, 1:2], scalar1=255.0,
                             scalar2=None, op0=OP.mult)
            ve.tensor_scalar(out=gb[:, 3:4], in0=gb[:, 0:1], scalar1=1.0 / 255.0,
                             scalar2=None, op0=OP.mult)
            ve.drain()
            ve.tensor_scalar(out=sb1[:], in0=sw1m[:], scalar1=gb[:, 3:4],
                             scalar2=1.0 / 127.0, op0=OP.mult, op1=OP.mult)
            ve.drain()
            ve.reciprocal(sA[:, 0:32], sb1[:])
            ve.drain()
            ve.tensor_tensor(out=sB[:, 0:32], in0=b1c[:], in1=sA[:, 0:32],
                             op=OP.mult)
            ve.drain()
            ve.tensor_copy(sci[:, 0:32], sB[:, 0:32])
            ve.drain()
            ve.tensor_copy(bq1[:], sci[:, 0:32])
            ve.drain()
            ve.tensor_tensor(out=eb1[:], in0=bq1[:], in1=sb1[:], op=OP.mult)
            ve.tensor_scalar(out=sB[:, 0:16], in0=srr[:, 32:48], scalar1=gb[:, 2:3],
                             scalar2=None, op0=OP.mult)
            ve.drain()
            ve.tensor_tensor(out=sDs[:, 0:16], in0=stm[:, 32:48], in1=sB[:, 0:16],
                             op=OP.mult)
            ve.drain()
            V.inc(ve.tensor_scalar(out=sC[:, 0:16], in0=sDs[:, 0:16], scalar1=-1.0,
                                   scalar2=None, op0=OP.mult), 1)
        v_p4 = V.n

        phK = ExitStack()
        k1T = phK.enter_context(nc.sbuf_tensor("k1T", [128, 8 * S], BF16))
        k1r = phK.enter_context(nc.sbuf_tensor("k1r", [128, 2 * D], BF16))

        a_k1v = []
        v_k1 = []
        for ti in range(NT):
            @blk.sync
            def _(sy, ti=ti):
                C["stx1"].wait(sy, 16 * NT)
                if ti == 0:
                    C["cc"].wait(sy, 1)
                if ti >= 2:
                    A.wait(sy, a_k1v[ti - 2])
                C["x1p%d" % (ti % 2)].inc(
                    sy.dma_start(out=x1rd[:, (ti % 2) * D:(ti % 2 + 1) * D],
                                 in_=x1_d[ti * 128:(ti + 1) * 128, :]), 16)

            @blk.scalar
            def _(sc, ti=ti):
                if ti == 0:
                    V.wait(sc, v_p4)
                C["x1p%d" % (ti % 2)].wait(sc, 16 * (ti // 2 + 1))
                if ti >= 2:
                    V.wait(sc, v_k1[ti - 2])
                A.inc(sc.activation(out=scr[:, (ti % 2) * 1024:(ti % 2) * 1024 + 1024],
                                    in_=x1rd[:, (ti % 2) * D:(ti % 2 + 1) * D],
                                    func=AF.Relu, bias=sC[:, ti:ti + 1],
                                    scale=sB[:, ti:ti + 1]), 1)
            a_k1v.append(A.n)

            @blk.vector
            def _(ve, ti=ti):
                A.wait(ve, a_k1v[ti])
                if ti >= 2:
                    C["stk1"].wait(ve, 16 * ti)
                ve.tensor_copy(sci[:, (ti % 2) * 1024:(ti % 2) * 1024 + 1024],
                               scr[:, (ti % 2) * 1024:(ti % 2) * 1024 + 1024])
                V.inc(ve.tensor_copy(k1r[:, (ti % 2) * D:(ti % 2 + 1) * D],
                                     sci[:, (ti % 2) * 1024:(ti % 2) * 1024 + 1024]), 1)
            v_k1.append(V.n)

            @blk.gpsimd
            def _(gp, ti=ti):
                V.wait(gp, v_k1[ti])
                C["stk1"].inc(gp.dma_start(out=k1_d[ti * 128:(ti + 1) * 128, :],
                                           in_=k1r[:, (ti % 2) * D:(ti % 2 + 1) * D]), 16)

        @blk.sync
        def _(sy):
            C["stk1"].wait(sy, 16 * NT)
            for dc in range(8):
                C["trk1"].inc(sy.dma_start_transpose(
                    k1T[:, dc * S:(dc + 1) * S],
                    k1_d[0:S, dc * 128:(dc + 1) * 128]), 16)

        ph6 = ExitStack()
        j1T = ph6.enter_context(nc.sbuf_tensor("j1T", [128, 8 * FF], BF16))
        w2r = ph6.enter_context(nc.sbuf_tensor("w2r", [128, 2 * FF], F32))
        j2r = ph6.enter_context(nc.sbuf_tensor("j2r", [128, 2 * FF], BF16))
        hr = ph6.enter_context(nc.sbuf_tensor("hr", [128, 4 * 512], F16))
        p6 = ph6.enter_context(nc.psum_tensor("p6", [128, 2048], F32))

        @blk.sync
        def _(sy):
            C["stj1"].wait(sy, 16 * 32)
            C["cc"].wait(sy, 1)
            for dc in range(8):
                C["trj1"].inc(sy.dma_start_transpose(
                    j1T[:, dc * FF:(dc + 1) * FF],
                    j1_d[0:FF, dc * 128:(dc + 1) * 128]), 16)

        v_w2q = [0] * 8
        t_h = []
        a_h = []
        v_hm = []
        idx = 0
        for fc in range(32):
            for tt in range(TT):
                bank = idx % 4

                @blk.tensor
                def _(te, fc=fc, tt=tt, bank=bank, idx=idx):
                    if idx == 0:
                        C["trk1"].wait(te, 16 * 8)
                        C["trj1"].wait(te, 16 * 8)
                    if idx >= 4:
                        A.wait(te, a_h[idx - 4])
                    for dc in range(8):
                        ins = te.matmul(
                            p6[:, bank * 512:(bank + 1) * 512],
                            j1T[:, dc * FF + fc * 128:dc * FF + fc * 128 + 128],
                            k1T[:, dc * S + tt * 512:dc * S + tt * 512 + 512],
                            start=(dc == 0), stop=(dc == 7))
                    T.inc(ins, 1)
                t_h.append(T.n)

                @blk.scalar
                def _(sc, fc=fc, bank=bank, idx=idx):
                    T.wait(sc, t_h[idx])
                    if idx >= 4:
                        C["sth"].wait(sc, 16 * idx)
                        V.wait(sc, v_hm[idx - 4])
                    A.inc(sc.activation(out=hr[:, (idx % 4) * 512:(idx % 4 + 1) * 512],
                                        in_=p6[:, bank * 512:(bank + 1) * 512],
                                        func=AF.Relu, bias=eb1[:, fc:fc + 1],
                                        scale=sb1[:, fc:fc + 1]), 1)
                a_h.append(A.n)

                @blk.vector
                def _(ve, idx=idx):
                    A.wait(ve, a_h[idx])
                    V.inc(ve.reduce_max(hm[:, idx:idx + 1],
                                        hr[:, (idx % 4) * 512:(idx % 4 + 1) * 512],
                                        axis=AX.X), 1)
                v_hm.append(V.n)

                @blk.sync
                def _(sy, fc=fc, tt=tt, idx=idx):
                    A.wait(sy, a_h[idx])
                    C["sth"].inc(sy.dma_start(
                        out=h_d[fc * 128:(fc + 1) * 128, tt * 512:(tt + 1) * 512],
                        in_=hr[:, (idx % 4) * 512:(idx % 4 + 1) * 512]), 16)
                idx += 1

        for oc in range(8):

            @blk.sync
            def _(sy, oc=oc):
                if oc >= 2:
                    V.wait(sy, v_w2q[oc - 2])
                C["w2s%d" % (oc % 2)].inc(sy.dma_start(
                    out=w2r[:, (oc % 2) * FF:(oc % 2) * FF + 2048],
                    in_=w2_in[oc * 128:(oc + 1) * 128, 0:2048]), 16)

            @blk.scalar
            def _(sc, oc=oc):
                if oc >= 2:
                    V.wait(sc, v_w2q[oc - 2])
                C["w2s%d" % (oc % 2)].inc(sc.dma_start(
                    out=w2r[:, (oc % 2) * FF + 2048:(oc % 2 + 1) * FF],
                    in_=w2_in[oc * 128:(oc + 1) * 128, 2048:4096]), 16)

            @blk.vector
            def _(ve, oc=oc):
                C["w2s%d" % (oc % 2)].wait(ve, 32 * (oc // 2 + 1))
                if oc >= 2:
                    C["stj2"].wait(ve, 16 * oc)
                wsl = w2r[:, (oc % 2) * FF:(oc % 2 + 1) * FF]
                ve.reduce_max(tny[:, 4:5], wsl, axis=AX.X,
                              apply_absolute_value=True)
                ve.drain()
                ve.tensor_scalar(out=sw2m[:, oc:oc + 1], in0=tny[:, 4:5],
                                 scalar1=EPS_Q, scalar2=None, op0=OP.max)
                ve.drain()
                ve.reciprocal(tny[:, 5:6], sw2m[:, oc:oc + 1])
                ve.drain()
                ve.tensor_scalar(out=tny[:, 6:7], in0=tny[:, 5:6],
                                 scalar1=127.0, scalar2=None, op0=OP.mult)
                ve.drain()
                for half in range(2):
                    ve.tensor_scalar(out=sci[:, (half % 2) * 2048:
                                         (half % 2) * 2048 + 2048][:, 0:2048]
                                     if False else sci[:, 0:2048],
                                     in0=wsl[:, half * 2048:(half + 1) * 2048],
                                     scalar1=tny[:, 6:7], scalar2=None,
                                     op0=OP.mult)
                    ins = ve.tensor_copy(
                        j2r[:, (oc % 2) * FF + half * 2048:
                            (oc % 2) * FF + half * 2048 + 2048],
                        sci[:, 0:2048])
                V.inc(ins, 1)
                v_w2q[oc] = V.n

            @blk.gpsimd
            def _(gp, oc=oc):
                V.wait(gp, v_w2q[oc])
                C["stj2"].inc(gp.dma_start(
                    out=j2_d[oc * 128:(oc + 1) * 128, :],
                    in_=j2r[:, (oc % 2) * FF:(oc % 2 + 1) * FF]), 16)

        ph6.close()
        phK.close()

        @blk.vector
        def _(ve):
            V.wait(ve, v_hm[-1])
            ve.drain()
            V.inc(ve.reduce_max(tny[:, 7:8], hm[:, 0:128], axis=AX.X), 1)
        v_lmax2 = V.n

        @blk.tensor
        def _(te):
            V.wait(te, v_lmax2)
            T.inc(te.transpose(pcc[0:1, :], tny[:, 7:8], ident[:]), 1)
        t_max2 = T.n

        @blk.vector
        def _(ve):
            T.wait(ve, t_max2)
            ve.reduce_max(gsc[0:1, 1:2], pcc[0:1, :], axis=AX.X)
            ve.drain()
            V.inc(ve.tensor_scalar(out=gsc[0:1, 0:1], in0=gsc[0:1, 1:2],
                                   scalar1=EPS_Q, scalar2=None, op0=OP.max), 1)
        v_g2 = V.n

        @blk.gpsimd
        def _(gp):
            V.wait(gp, v_g2)
            C["sth"].wait(gp, 16 * 128)
            C["stj2"].wait(gp, 16 * 8)
            C["ccd"].inc(gp.dma_start(out=cc2_i[:], in_=gsc[:]), 16)
            C["ccd"].wait(gp)
            C["cc"].inc(gp.collective_compute(
                "AllReduce", OP.max, replica_groups=RG,
                ins=[cc2_i[:]], outs=[cc2_o[:]]), 1)
            C["cc"].wait(gp)
            C["ccd"].inc(gp.dma_start(out=gsc[0:1, 2:3], in_=cc2_o[0:1, 0:1]), 16)
            G.inc(gp.nop(), 1)
        g_cc2 = G.n

        @blk.tensor
        def _(te):
            C["ccd"].wait(te)
            G.wait(te, g_cc2)
            T.inc(te.matmul(pbc[:, 1:2], ones[:], gsc[0:1, 2:3],
                            start=True, stop=True), 1)
        t_bc2 = T.n


        ph7 = ExitStack()
        j2T = ph7.enter_context(nc.sbuf_tensor("j2T", [128, 32 * 1024], BF16))
        k2b = ph7.enter_context(nc.sbuf_tensor("k2b", [128, 2 * 32 * 512], BF16))
        h2r = ph7.enter_context(nc.sbuf_tensor("h2r", [128, 2 * 2048], F16))
        o2tm = ph7.enter_context(nc.sbuf_tensor("o2tm", [128, 4 * D], BF16))
        x1r4 = ph7.enter_context(nc.sbuf_tensor("x1r4", [128, 4 * D], F32))
        outr = ph7.enter_context(nc.sbuf_tensor("outr", [128, 2 * D], F32))
        o2r = ph7.enter_context(nc.sbuf_tensor("o2r", [128, 2 * 512], BF16))
        p7 = ph7.enter_context(nc.psum_tensor("p7", [128, 2048], F32))

        @blk.vector
        def _(ve):
            T.wait(ve, t_bc2)
            ve.tensor_copy(gb[:, 4:5], pbc[:, 1:2])
            ve.drain()
            ve.reciprocal(gb[:, 5:6], gb[:, 4:5])
            ve.drain()
            ve.tensor_scalar(out=gb[:, 6:7], in0=gb[:, 5:6], scalar1=255.0,
                             scalar2=None, op0=OP.mult)
            V.inc(ve.tensor_scalar(out=gb[:, 7:8], in0=gb[:, 4:5],
                                   scalar1=1.0 / 255.0,
                                   scalar2=None, op0=OP.mult), 1)
        v_p7a = V.n

        @blk.vector
        def _(ve):
            V.wait(ve, v_p7a)
            ve.drain()
            ve.tensor_scalar(out=sb2[:], in0=sw2m[:], scalar1=gb[:, 7:8],
                             scalar2=1.0 / 127.0, op0=OP.mult, op1=OP.mult)
            ve.drain()
            ve.reciprocal(sA[:, 0:8], sb2[:])
            ve.drain()
            ve.tensor_tensor(out=sB[:, 0:8], in0=b2c[:], in1=sA[:, 0:8], op=OP.mult)
            ve.drain()
            ve.tensor_copy(sci[:, 0:8], sB[:, 0:8])
            ve.drain()
            ve.tensor_copy(bq2[:], sci[:, 0:8])
            ve.drain()
            V.inc(ve.tensor_tensor(out=eb2[:], in0=bq2[:], in1=sb2[:], op=OP.mult), 1)
        v_p7 = V.n


        @blk.sync
        def _(sy):
            C["stj2"].wait(sy, 16 * 8)
            C["cc"].wait(sy, 2)
            for fc in range(32):
                C["trj2"].inc(sy.dma_start_transpose(
                    j2T[:, fc * 1024:(fc + 1) * 1024],
                    j2_d[0:D, fc * 128:(fc + 1) * 128]), 16)

        def emit_P8(tt):
                @blk.scalar
                def _(sc, tt=tt):
                    C["sto2"].wait(sc, 16 * 8 * (tt + 1))
                    if tt > 0:
                        V.wait(sc, v_outc[-1])
                        A.wait(sc, A.n)
                    for tc in range(4):
                        ti = tt * 4 + tc
                        C["tro2"].inc(sc.dma_start_transpose(
                            o2tm[:, tc * D:(tc + 1) * D],
                            o2_d[0:D, ti * 128:(ti + 1) * 128]), 16)
                        C["x1p%d" % (ti % 2)].inc(sc.dma_start(
                            out=x1r4[:, tc * D:(tc + 1) * D],
                            in_=x1_d[ti * 128:(ti + 1) * 128, :]), 16)

                @blk.vector
                def _(ve, tt=tt):
                    C["tro2"].wait(ve, 16 * 4 * (tt + 1))
                    for tc in range(4):
                        ins = ve.reduce_sum(ssum[:, 48 + tt * 4 + tc:49 + tt * 4 + tc],
                                            o2tm[:, tc * D:(tc + 1) * D], axis=AX.X)
                    V.inc(ins, 1)
                v_ln4sum = V.n

                @blk.vector
                def _(ve, tt=tt):
                    C["tro2"].wait(ve, 16 * 4 * (tt + 1))
                    for tc in range(4):
                        ti = tt * 4 + tc
                        src_ = o2tm[:, tc * D:(tc + 1) * D]
                        ve.tensor_tensor(out=scr[:, 0:1024], in0=src_, in1=src_,
                                         op=OP.mult)
                        ve.reduce_sum(ssum[:, 48 + ti:49 + ti], src_, axis=AX.X)
                        ins = ve.reduce_sum(ssq[:, 48 + ti:49 + ti], scr[:, 0:1024],
                                            axis=AX.X)
                    V.inc(ins, 1)

                v_ln4fin = ln_finalize(48 + tt * 4, 4)

                for tc in range(4):
                    ti = tt * 4 + tc

                    @blk.vector
                    def _(ve, tt=tt, tc=tc, ti=ti):
                        if tc == 0:
                            V.wait(ve, v_ln4fin)
                            ve.drain()
                        C["x1p%d" % (ti % 2)].wait(ve, 16 * (8 + ti // 2 + 1))
                        if len(v_outc) >= 2:
                            C["stout"].wait(ve, 16 * len(v_outc))
                        ve.tensor_scalar(out=scr[:, 1024:2048],
                                         in0=o2tm[:, tc * D:(tc + 1) * D],
                                         scalar1=stm[:, 48 + ti:49 + ti],
                                         scalar2=srr[:, 48 + ti:49 + ti],
                                         op0=OP.subtract, op1=OP.mult)
                        V.inc(ve.tensor_tensor(out=outr[:, (tc % 2) * D:(tc % 2 + 1) * D],
                                               in0=x1r4[:, tc * D:(tc + 1) * D],
                                               in1=scr[:, 1024:2048], op=OP.add), 1)
                    v_outc.append(V.n)

                    @blk.gpsimd
                    def _(gp, ti=ti, tc=tc):
                        V.wait(gp, v_outc[-1])
                        C["stout"].inc(gp.dma_start(
                            out=y_out[ti * 128:(ti + 1) * 128, :],
                            in_=outr[:, (tc % 2) * D:(tc % 2 + 1) * D]), 16)



        t_o2 = {}
        a_o2 = {}
        v_k2g = {}
        v_outc = []
        for tt in range(TT):
            for g in range(8):
                gi = tt * 8 + g

                @blk.gpsimd
                def _(gp, tt=tt, g=g, gi=gi):
                    C["sth"].wait(gp, 16 * 128)
                    if gi == 0:
                        C["cc"].wait(gp, 2)
                    if gi >= 2:
                        V.wait(gp, v_k2g[gi - 2])
                    C["ldhp%d" % (gi % 2)].inc(gp.dma_start(
                        out=h2r[:, (gi % 2) * 2048:(gi % 2 + 1) * 2048],
                        in_=h_d[g * 512:(g + 1) * 512,
                                tt * 512:(tt + 1) * 512].rearrange(
                                    "(c p) t -> p c t", p=128)), 16)

                @blk.vector
                def _(ve, tt=tt, g=g, gi=gi):
                    if gi == 0:
                        V.wait(ve, v_p7)
                    C["ldhp%d" % (gi % 2)].wait(ve, 16 * (gi // 2 + 1))
                    if tt >= 2 and g == 0:
                        T.wait(ve, t_o2[(tt - 2, 7)])
                    ve.tensor_scalar(out=sci[:, 0:2048],
                                     in0=h2r[:, (gi % 2) * 2048:(gi % 2 + 1) * 2048],
                                     scalar1=gb[:, 6:7], scalar2=None, op0=OP.mult)
                    ve.drain()
                    ins = ve.tensor_copy(
                        k2b[:, (tt % 2) * 16384 + g * 2048:
                            (tt % 2) * 16384 + (g + 1) * 2048],
                        sci[:, 0:2048])
                    ve.drain()
                    V.inc(ins, 1)
                v_k2g[gi] = V.n

            for do in range(8):
                o2i = tt * 8 + do
                bank = o2i % 4

                @blk.tensor
                def _(te, tt=tt, do=do, bank=bank, o2i=o2i):
                    if tt == 0 and do == 0:
                        A.wait(te, a_h[-1])
                    if do == 0:
                        V.wait(te, v_k2g[tt * 8 + 7])
                        C["trj2"].wait(te, 16 * 32)
                    if o2i >= 4:
                        V.wait(te, a_o2[o2i - 4])
                    for fc in range(32):
                        ins = te.matmul(
                            p7[:, bank * 512:(bank + 1) * 512],
                            j2T[:, fc * 1024 + do * 128:fc * 1024 + do * 128 + 128],
                            k2b[:, (tt % 2) * 16384 + fc * 512:
                                (tt % 2) * 16384 + fc * 512 + 512],
                            start=(fc == 0), stop=(fc == 31))
                    T.inc(ins, 1)
                t_o2[(tt, do)] = T.n

                @blk.vector
                def _(ve, tt=tt, do=do, bank=bank, o2i=o2i):
                    T.wait(ve, t_o2[(tt, do)])
                    if o2i >= 2:
                        C["sto2"].wait(ve, 16 * o2i)
                    V.inc(ve.tensor_scalar(
                        out=o2r[:, (do % 2) * 512:(do % 2 + 1) * 512],
                        in0=p7[:, bank * 512:(bank + 1) * 512],
                        scalar1=bq2[:, do:do + 1], scalar2=sb2[:, do:do + 1],
                        op0=OP.add, op1=OP.mult), 1)
                a_o2[o2i] = V.n

                @blk.sync
                def _(sy, tt=tt, do=do):
                    V.wait(sy, a_o2[tt * 8 + do])
                    C["sto2"].inc(sy.dma_start(
                        out=o2_d[do * 128:(do + 1) * 128, tt * 512:(tt + 1) * 512],
                        in_=o2r[:, (do % 2) * 512:(do % 2 + 1) * 512]), 16)

            if tt >= 1:
                emit_P8(tt - 1)

        emit_P8(TT - 1)

        @blk.gpsimd
        def _(gp):
            C["stout"].wait(gp, 16 * NT)
            C["stout"].inc(gp.dma_start(out=dbg_d[:, 0:64], in_=ssum[:]), 16)
            C["stout"].inc(gp.dma_start(out=dbg_d[:, 64:128], in_=ssq[:]), 16)
            C["stout"].inc(gp.dma_start(out=dbg_d[:, 128:192], in_=stm[:]), 16)
            C["stout"].inc(gp.dma_start(out=dbg_d[:, 192:256], in_=srr[:]), 16)
            if DEBUG_OUTPUTS:
                C["stout"].inc(gp.dma_start(out=dbg2_d[:, 192:200], in_=sw2m[:]), 16)
                C["stout"].inc(gp.dma_start(out=dbg2_d[:, 200:208], in_=sb2[:]), 16)
                C["stout"].inc(gp.dma_start(out=dbg2_d[:, 208:216], in_=gb[:]), 16)
                C["stout"].wait(gp, 16 * NT + 64 + 48)
            C["stout"].wait(gp, 16 * NT + 64)

        ph7.close()
        phF.close()

        for name, ctr in C.items():
            assert ctr.max_waited <= ctr.n, (name, ctr.max_waited, ctr.n)

    return nc


_NC_CACHE = {}


def kernel(L0_attn_hidden_states, L0_attn_queries, L0_attn_combiners,
           L0_ffn_w1, L0_ffn_b1, L0_ffn_w2, L0_ffn_b2):
    x = np.ascontiguousarray(np.asarray(L0_attn_hidden_states, dtype=np.float32))
    qw = np.ascontiguousarray(np.asarray(L0_attn_queries, dtype=np.float32))
    cb = np.ascontiguousarray(np.asarray(L0_attn_combiners, dtype=np.float32))
    w1 = np.ascontiguousarray(np.asarray(L0_ffn_w1, dtype=np.float32))
    b1 = np.ascontiguousarray(np.asarray(L0_ffn_b1, dtype=np.float32))
    w2 = np.ascontiguousarray(np.asarray(L0_ffn_w2, dtype=np.float32))
    b2 = np.ascontiguousarray(np.asarray(L0_ffn_b2, dtype=np.float32))

    if "nc" not in _NC_CACHE:
        _NC_CACHE["nc"] = build_nc()
    nc = _NC_CACHE["nc"]

    qwT = np.ascontiguousarray(qw.transpose(0, 2, 1))
    in_maps = [{"x0": x[b], "qwT": qwT, "cb": cb, "w1": w1, "b1": b1,
                "w2": w2, "b2": b2} for b in range(8)]
    res = run_bass_kernel_spmd(nc, in_maps, core_ids=list(range(8)))
    return np.stack([np.asarray(r["y"], dtype=np.float32) for r in res.results])


# version 24
# speedup vs baseline: 1.0432x; 1.0432x over previous
import numpy as np
from contextlib import ExitStack
import concourse.bass as bass
import concourse.mybir as mybir
from concourse.bass_utils import run_bass_kernel_spmd

F32 = mybir.dt.float32
BF16 = mybir.dt.bfloat16
F16 = mybir.dt.float16
I32 = mybir.dt.int32
AF = mybir.ActivationFunctionType
OP = mybir.AluOpType
AX = mybir.AxisListType

S, D, FF = 2048, 1024, 4096
NT, TT = 16, 4
NA, H = 4, 256
EPS_LN, EPS_Q = 1e-5, 1e-8
RG = [[0, 1, 2, 3, 4, 5, 6, 7]]
INV_D = 1.0 / D


class Ctr:
    def __init__(self, sem):
        self.sem = sem
        self.n = 0
        self.max_waited = 0

    def inc(self, ins, k):
        ins.then_inc(self.sem, k)
        self.n += k
        return self.n

    def wait(self, eng, val=None):
        if val is None:
            val = self.n
        if val > 0:
            eng.wait_ge(self.sem, val)
            self.max_waited = max(self.max_waited, val)


DEBUG_OUTPUTS = False


def build_nc():
    nc = bass.Bass("TRN2", target_bir_lowering=False)
    dbg = "ExternalOutput" if DEBUG_OUTPUTS else "Internal"

    x_in = nc.dram_tensor("x0", [S, D], F32, kind="ExternalInput").ap()
    qt_in = nc.dram_tensor("qwT", [NA, D, H], F32, kind="ExternalInput").ap()
    c_in = nc.dram_tensor("cb", [NA, D, H], F32, kind="ExternalInput").ap()
    w1_in = nc.dram_tensor("w1", [FF, D], F32, kind="ExternalInput").ap()
    b1_in = nc.dram_tensor("b1", [FF], F32, kind="ExternalInput").ap()
    w2_in = nc.dram_tensor("w2", [D, FF], F32, kind="ExternalInput").ap()
    b2_in = nc.dram_tensor("b2", [D], F32, kind="ExternalInput").ap()
    y_out = nc.dram_tensor("y", [S, D], F32, kind="ExternalOutput").ap()

    ln1_d = nc.dram_tensor("ln1_d", [S, D], BF16, kind=dbg).ap()
    x1_d = nc.dram_tensor("x1_d", [S, D], F32, kind=dbg).ap()
    k1_d = nc.dram_tensor("k1_d", [S, D], BF16, kind=dbg).ap()
    ao_d = nc.dram_tensor("ao_d", [D, S], BF16, kind=dbg).ap()
    o2_d = nc.dram_tensor("o2_d", [D, S], BF16, kind=dbg).ap()
    j1_d = nc.dram_tensor("j1_d", [FF, D], BF16, kind=dbg).ap()
    j2_d = nc.dram_tensor("j2_d", [D, FF], BF16, kind=dbg).ap()
    h_d = nc.dram_tensor("h_d", [FF, S], F16, kind=dbg).ap()
    dbg_d = nc.dram_tensor("dbg_d", [128, 256], F32, kind=dbg).ap()
    dbg2_d = nc.dram_tensor("dbg2_d", [128, 256], F32, kind=dbg).ap()
    dbg3_d = nc.dram_tensor("dbg3_d", [128, 4096], BF16, kind=dbg).ap()
    cc1_i = nc.dram_tensor("cc1_i", [1, 16], F32, kind="Internal").ap()
    cc1_o = nc.dram_tensor("cc1_o", [1, 16], F32, kind="Internal").ap()
    cc2_i = nc.dram_tensor("cc2_i", [1, 16], F32, kind="Internal").ap()
    cc2_o = nc.dram_tensor("cc2_o", [1, 16], F32, kind="Internal").ap()

    top = ExitStack()
    with top:
        def sbuf(name, shape, dt):
            return top.enter_context(nc.sbuf_tensor(name, shape, dt))

        stm = sbuf("stm", [128, 64], F32)
        srr = sbuf("srr", [128, 64], F32)
        sA = sbuf("sA", [128, 64], F32)
        sB = sbuf("sB", [128, 64], F32)
        sC = sbuf("sC", [128, 64], F32)
        sDs = sbuf("sDs", [128, 64], F32)
        ssum = sbuf("ssum", [128, 64], F32)
        ssq = sbuf("ssq", [128, 64], F32)
        rx = sbuf("rx", [128, 16], F32)
        hm = sbuf("hm", [128, 128], F32)
        sw1m = sbuf("sw1m", [128, 32], F32)
        sb1 = sbuf("sb1", [128, 32], F32)
        eb1 = sbuf("eb1", [128, 32], F32)
        bq1 = sbuf("bq1", [128, 32], F32)
        sw2m = sbuf("sw2m", [128, 8], F32)
        sb2 = sbuf("sb2", [128, 8], F32)
        eb2 = sbuf("eb2", [128, 8], F32)
        bq2 = sbuf("bq2", [128, 8], F32)
        b1c = sbuf("b1c", [128, 32], F32)
        b2c = sbuf("b2c", [128, 8], F32)
        ident = sbuf("ident", [128, 128], F32)
        identb = sbuf("identb", [128, 128], BF16)
        io1 = sbuf("io1", [128, 128], F32)
        io2 = sbuf("io2", [128, 1], F32)
        ones = sbuf("ones", [1, 128], F32)
        gb = sbuf("gb", [128, 8], F32)
        gsc = sbuf("gsc", [1, 16], F32)
        scr = sbuf("scr", [128, 2048], F32)
        sci = sbuf("sci", [128, 2048], I32)
        tny = sbuf("tny", [128, 8], F32)

        C = {}
        for name in ["w1s0", "w1s1", "w1s2", "w1s3", "w1s4", "w1s5", "w1s6", "w1s7",
                     "w2s0", "w2s1", "x1p0", "x1p1", "ldhp0", "ldhp1",
                     "ldx", "ldq", "stln", "trln",
                     "stao", "trao", "stx1", "stk1", "trk1",
                     "stj1", "trj1", "stj2", "trj2", "sth",
                     "sto2", "tro2", "stout", "cc", "ccd",
                     "V", "A", "T", "G"]:
            C[name] = Ctr(top.enter_context(nc.semaphore(name)))
        V, A, T, G = C["V"], C["A"], C["T"], C["G"]

        blk = top.enter_context(nc.Block())

        ph_x = ExitStack()
        xres = ph_x.enter_context(nc.sbuf_tensor("xres", [128, NT * D], F32))
        sqs = ph_x.enter_context(nc.sbuf_tensor("sqs", [128, 2048], F32))
        p01 = ExitStack()
        pb = p01.enter_context(nc.psum_tensor("pb", [128, 512], F32))
        pm = [p01.enter_context(nc.psum_tensor("pm%d" % q, [128, 512], F32))
              for q in range(NA)]

        @blk.sync
        def _(sy):
            for hh in range(2):
                C["ldx"].inc(sy.dma_start(
                    out=xres[:, hh * 8 * D:(hh + 1) * 8 * D],
                    in_=x_in[hh * 1024:(hh + 1) * 1024, :].rearrange(
                        "(t p) d -> p t d", p=128)), 16)

        attn = ExitStack()
        qtb = attn.enter_context(nc.sbuf_tensor("qtb", [128, 32 * H], BF16))
        wpp = attn.enter_context(nc.sbuf_tensor("wpp", [128, 16 * 512], BF16))
        cbf = attn.enter_context(nc.sbuf_tensor("cbf", [128, 32 * H], BF16))
        lnr = attn.enter_context(nc.sbuf_tensor("lnr", [128, 2 * D], BF16))
        Mbf = attn.enter_context(nc.sbuf_tensor("Mbf", [128, 8 * H], BF16))
        ln1T = attn.enter_context(nc.sbuf_tensor("ln1T", [128, 8 * 512], BF16))
        atb = attn.enter_context(nc.sbuf_tensor("atb", [128, 2 * 8 * 512], BF16))
        aob = attn.enter_context(nc.sbuf_tensor("aob", [128, 2 * 512], BF16))
        aotm = attn.enter_context(nc.sbuf_tensor("aotm", [128, 4 * D], BF16))
        w1r = attn.enter_context(nc.sbuf_tensor("w1r", [128, 4 * D], F32))
        j1r = attn.enter_context(nc.sbuf_tensor("j1r", [128, 2 * D], BF16))

        @blk.gpsimd
        def _(gp):
            gp.iota(io1[:], pattern=[[1, 128]], base=0, channel_multiplier=0,
                    allow_small_or_imprecise_dtypes=True)
            G.inc(gp.iota(io2[:], pattern=[[0, 1]], base=0, channel_multiplier=1,
                          allow_small_or_imprecise_dtypes=True), 1)
            for a in range(NA):
                for r in range(8):
                    C["ldq"].inc(gp.dma_start(
                        out=qtb[:, (a * 8 + r) * H:(a * 8 + r + 1) * H],
                        in_=qt_in[a, r * 128:(r + 1) * 128, :]), 16)
            for a in range(NA):
                for kc in range(8):
                    C["ldq"].inc(gp.dma_start(
                        out=cbf[:, (a * 8 + kc) * H:(a * 8 + kc + 1) * H],
                        in_=c_in[a, kc * 128:(kc + 1) * 128, :]), 16)
            C["ldq"].inc(gp.dma_start(
                out=scr[0:32, 0:128],
                in_=b1_in.rearrange("(a b) -> a b", b=128)), 16)
            C["ldq"].inc(gp.dma_start(
                out=scr[0:8, 128:256],
                in_=b2_in.rearrange("(a b) -> a b", b=128)), 16)

        @blk.vector
        def _(ve):
            G.wait(ve)
            ve.tensor_scalar(out=ident[:], in0=io1[:], scalar1=io2[:, 0:1],
                             scalar2=None, op0=OP.is_equal)
            ve.memset(ones[:], 1.0)
            ve.drain()
            ve.tensor_copy(identb[:], ident[:])
            V.inc(ve.memset(gsc[:], 0.0), 1)
        v_ident = V.n

        @blk.tensor
        def _(te):
            C["ldq"].wait(te)
            V.wait(te, v_ident)
            te.transpose(pb[:, 0:32], scr[0:32, 0:128], ident[0:32, 0:32])
            T.inc(te.transpose(pb[:, 32:40], scr[0:8, 128:256], ident[0:8, 0:8]), 1)
        t_btr = T.n

        @blk.vector
        def _(ve):
            T.wait(ve, t_btr)
            ve.tensor_copy(b1c[:], pb[:, 0:32])
            ve.drain()
            V.inc(ve.tensor_copy(b2c[:], pb[:, 32:40]), 1)
        v_bc = V.n

        @blk.vector
        def _(ve):
            for ti in range(NT):
                if ti % 8 == 0:
                    C["ldx"].wait(ve, 16 * (ti // 8 + 1))
                xt_ = xres[:, ti * D:(ti + 1) * D]
                ve.tensor_tensor(out=sqs[:, 0:1024], in0=xt_, in1=xt_, op=OP.mult)
                ve.reduce_sum(ssum[:, ti:ti + 1], xt_, axis=AX.X)
                ins = ve.reduce_sum(ssq[:, ti:ti + 1], sqs[:, 0:1024], axis=AX.X)
            V.inc(ins, 1)
        v_ln1sum = V.n
        a_ln1sq = A.n

        def ln_finalize(lo, n):
            h = n // 2
            sla, slb = slice(lo, lo + h), slice(lo + h, lo + n)
            sl = slice(lo, lo + n)

            @blk.vector
            def _(ve):
                ve.drain()
                ve.tensor_scalar(out=stm[:, sla], in0=ssum[:, sla], scalar1=INV_D,
                                 scalar2=None, op0=OP.mult)
                ve.tensor_scalar(out=stm[:, slb], in0=ssum[:, slb], scalar1=INV_D,
                                 scalar2=None, op0=OP.mult)
                ve.tensor_scalar(out=sA[:, sla], in0=ssq[:, sla], scalar1=INV_D,
                                 scalar2=EPS_LN, op0=OP.mult, op1=OP.add)
                ve.tensor_scalar(out=sA[:, slb], in0=ssq[:, slb], scalar1=INV_D,
                                 scalar2=EPS_LN, op0=OP.mult, op1=OP.add)
                ve.tensor_tensor(out=sB[:, sla], in0=stm[:, sla], in1=stm[:, sla],
                                 op=OP.mult)
                ve.tensor_tensor(out=sB[:, slb], in0=stm[:, slb], in1=stm[:, slb],
                                 op=OP.mult)
                ve.tensor_tensor(out=sDs[:, sla], in0=sA[:, sla], in1=sB[:, sla],
                                 op=OP.subtract)
                V.inc(ve.tensor_tensor(out=sDs[:, slb], in0=sA[:, slb],
                                       in1=sB[:, slb], op=OP.subtract), 1)
            v1 = V.n

            @blk.scalar
            def _(sc):
                V.wait(sc, v1)
                A.inc(sc.activation(out=sB[:, sl], in_=sDs[:, sl], func=AF.Sqrt), 1)
            a1 = A.n

            @blk.vector
            def _(ve):
                A.wait(ve, a1)
                ve.drain()
                ve.reciprocal(sC[:, sla], sB[:, sla])
                ve.reciprocal(sC[:, slb], sB[:, slb])
                ve.tensor_tensor(out=sA[:, sla], in0=sC[:, sla], in1=sC[:, sla],
                                 op=OP.mult)
                ve.tensor_tensor(out=sA[:, slb], in0=sC[:, slb], in1=sC[:, slb],
                                 op=OP.mult)
                ve.tensor_tensor(out=sB[:, sla], in0=sDs[:, sla], in1=sA[:, sla],
                                 op=OP.mult)
                ve.tensor_tensor(out=sB[:, slb], in0=sDs[:, slb], in1=sA[:, slb],
                                 op=OP.mult)
                ve.tensor_scalar(out=sA[:, sla], in0=sB[:, sla], scalar1=-0.5,
                                 scalar2=1.5, op0=OP.mult, op1=OP.add)
                ve.tensor_scalar(out=sA[:, slb], in0=sB[:, slb], scalar1=-0.5,
                                 scalar2=1.5, op0=OP.mult, op1=OP.add)
                ve.tensor_tensor(out=srr[:, sla], in0=sC[:, sla], in1=sA[:, sla],
                                 op=OP.mult)
                V.inc(ve.tensor_tensor(out=srr[:, slb], in0=sC[:, slb],
                                       in1=sA[:, slb], op=OP.mult), 1)
            return V.n

        v_ln1finA = ln_finalize(0, 8)
        v_ln1fin = ln_finalize(8, 8)

        if DEBUG_OUTPUTS:
            @blk.gpsimd
            def _(gp):
                V.wait(gp, v_ln1fin)
                C["ccd"].inc(gp.dma_start(out=dbg2_d[:, 0:64], in_=srr[:]), 16)
                C["ccd"].inc(gp.dma_start(out=dbg2_d[:, 64:128], in_=sC[:]), 16)
                C["ccd"].inc(gp.dma_start(out=dbg2_d[:, 192:256], in_=sDs[:]), 16)
                C["ccd"].wait(gp)

        v_ln1ap = []
        t_gram = []
        for ti in range(NT):
            slot = ti % 2

            @blk.vector
            def _(ve, ti=ti, slot=slot):
                if ti == 0:
                    V.wait(ve, v_ln1finA)
                    ve.drain()
                if ti == 8:
                    V.wait(ve, v_ln1fin)
                    ve.drain()
                if ti >= 2:
                    T.wait(ve, t_gram[ti - 2])
                    C["stln"].wait(ve, 16 * (ti - 1))
                V.inc(ve.tensor_scalar(out=lnr[:, slot * D:(slot + 1) * D],
                                       in0=xres[:, ti * D:(ti + 1) * D],
                                       scalar1=stm[:, ti:ti + 1],
                                       scalar2=srr[:, ti:ti + 1],
                                       op0=OP.subtract, op1=OP.mult), 1)
            v_ln1ap.append(V.n)

            @blk.tensor
            def _(te, ti=ti, slot=slot):
                V.wait(te, v_ln1ap[ti])
                for q in range(NA):
                    base = slot * D + q * H
                    for fc in range(2):
                        ins = te.matmul(pm[q][:, fc * H:(fc + 1) * H],
                                        lnr[:, base + fc * 128:base + fc * 128 + 128],
                                        lnr[:, base:base + H],
                                        start=(ti == 0 and fc == 0),
                                        stop=(ti == NT - 1))
                T.inc(ins, 1)
            t_gram.append(T.n)

            @blk.gpsimd
            def _(gp, ti=ti, slot=slot):
                V.wait(gp, v_ln1ap[ti])
                C["stln"].inc(gp.dma_start(out=ln1_d[ti * 128:(ti + 1) * 128, :],
                                           in_=lnr[:, slot * D:(slot + 1) * D]), 16)

        @blk.vector
        def _(ve):
            T.wait(ve, t_gram[-1])
            for q in range(NA):
                for fc in range(2):
                    ins = ve.tensor_copy(Mbf[:, (q * 2 + fc) * H:(q * 2 + fc + 1) * H],
                                         pm[q][:, fc * H:(fc + 1) * H])
            V.inc(ins, 1)
        v_mevac = V.n

        if DEBUG_OUTPUTS:
            @blk.gpsimd
            def _(gp):
                V.wait(gp, v_mevac)
                C["ccd"].inc(gp.dma_start(out=dbg3_d[:, 0:2048], in_=Mbf[:]), 16)
                C["ccd"].inc(gp.dma_start(out=dbg2_d[:, 128:192], in_=srr[:]), 16)
                C["ccd"].wait(gp)

        p01.close()
        p2 = ExitStack()
        pat = p2.enter_context(nc.psum_tensor("pat", [128, 2048], F32))
        po = p2.enter_context(nc.psum_tensor("po", [128, 2048], F32))

        t_wpp = []
        a_wpp = []
        for i in range(16):
            a, q = i // 4, i % 4
            boff = (i % 2) * 512

            @blk.tensor
            def _(te, a=a, q=q, boff=boff, i=i):
                if i == 0:
                    V.wait(te, v_mevac)
                    C["ldq"].wait(te)
                if i >= 2:
                    A.wait(te, a_wpp[i - 2])
                for ec in range(2):
                    for fc in range(2):
                        ins = te.matmul(
                            pat[:, boff + ec * H:boff + ec * H + H],
                            qtb[:, (a * 8 + q * 2 + fc) * H + ec * 128:
                                (a * 8 + q * 2 + fc) * H + ec * 128 + 128],
                            Mbf[:, (q * 2 + fc) * H:(q * 2 + fc + 1) * H],
                            start=(ec == 0 and fc == 0), stop=(ec == 1 and fc == 1))
                T.inc(ins, 1)
            t_wpp.append(T.n)

            @blk.scalar
            def _(sc, i=i, boff=boff):
                T.wait(sc, t_wpp[i])
                A.inc(sc.copy(out=wpp[:, i * 512:(i + 1) * 512],
                              in_=pat[:, boff:boff + 512]), 1)
            a_wpp.append(A.n)

        t_at = {}
        a_at = {}
        a_at_par = {}
        t_op = {}
        t_op_par = {}
        v_ao = {}
        v_ao_par = {}
        t_lastAT = [0] * TT
        v_w1q = [0] * 32
        v_ln2ap = {}

        def emit_AT(tt, a, q):
            @blk.tensor
            def _(te):
                if a == 0 and q == 0:
                    C["trln"].wait(te, 16 * 8 * (tt + 1))
                    if tt == 0:
                        A.wait(te, a_wpp[-1])
                pk = q % 2
                if pk in a_at_par:
                    eng, cnt = a_at_par[pk]
                    C[eng].wait(te, cnt)
                for gc in range(2):
                    for ec in range(2):
                        ins = te.matmul(
                            pat[:, pk * 1024 + gc * 512:pk * 1024 + gc * 512 + 512],
                            wpp[:, (a * 4 + q) * 512 + ec * H + gc * 128:
                                (a * 4 + q) * 512 + ec * H + gc * 128 + 128],
                            ln1T[:, (a * 2 + ec) * 512:(a * 2 + ec + 1) * 512],
                            start=(ec == 0), stop=(ec == 1))
                T.inc(ins, 1)
            t_at[(tt, a, q)] = T.n
            if a == NA - 1 and q == NA - 1:
                t_lastAT[tt] = T.n

            @blk.scalar
            def _(sc):
                T.wait(sc, t_at[(tt, a, q)])
                ap = a % 2
                if ap in t_op_par:
                    T.wait(sc, t_op_par[ap])
                pk = q % 2
                for gc in range(2):
                    ins = sc.copy(
                        out=atb[:, ap * 4096 + (q * 2 + gc) * 512:
                                ap * 4096 + (q * 2 + gc + 1) * 512],
                        in_=pat[:, pk * 1024 + gc * 512:pk * 1024 + gc * 512 + 512])
                A.inc(ins, 1)
            a_at[(tt, a, q)] = ("A", A.n)
            a_at_par[q % 2] = a_at[(tt, a, q)]

        def emit_OP(tt, a):
            @blk.tensor
            def _(te):
                A.wait(te, a_at[(tt, a, 3)][1])
                ap = a % 2
                if ap in v_ao_par:
                    V.wait(te, v_ao_par[ap])
                for oc in range(2):
                    for kc in range(8):
                        ins = te.matmul(
                            po[:, ap * 1024 + oc * 512:ap * 1024 + oc * 512 + 512],
                            cbf[:, (a * 8 + kc) * H + oc * 128:
                                (a * 8 + kc) * H + oc * 128 + 128],
                            atb[:, ap * 4096 + kc * 512:ap * 4096 + (kc + 1) * 512],
                            start=(kc == 0), stop=(kc == 7))
                T.inc(ins, 1)
            t_op[(tt, a)] = T.n
            t_op_par[a % 2] = T.n

            @blk.vector
            def _(ve):
                T.wait(ve, t_op[(tt, a)])
                n_prev = len(v_ao)
                if n_prev >= 1:
                    C["stao"].wait(ve, 16 * 2 * n_prev)
                ap = a % 2
                for oc in range(2):
                    ins = ve.tensor_copy(aob[:, oc * 512:(oc + 1) * 512],
                                         po[:, ap * 1024 + oc * 512:
                                             ap * 1024 + oc * 512 + 512])
                V.inc(ins, 1)
            v_ao[(tt, a)] = V.n
            v_ao_par[a % 2] = V.n

            @blk.gpsimd
            def _(gp):
                V.wait(gp, v_ao[(tt, a)])
                for oc in range(2):
                    C["stao"].inc(gp.dma_start(
                        out=ao_d[a * H + oc * 128:a * H + oc * 128 + 128,
                                 tt * 512:(tt + 1) * 512],
                        in_=aob[:, oc * 512:(oc + 1) * 512]), 16)

        for tt in range(TT):
            @blk.sync
            def _(sy, tt=tt):
                C["stln"].wait(sy, 16 * NT)
                if tt >= 1:
                    T.wait(sy, t_lastAT[tt - 1])
                for ec in range(8):
                    C["trln"].inc(sy.dma_start_transpose(
                        ln1T[:, ec * 512:(ec + 1) * 512],
                        ln1_d[tt * 512:(tt + 1) * 512, ec * 128:(ec + 1) * 128]), 16)
                for fc in range(tt * 8, tt * 8 + 4):
                    if fc >= 4:
                        V.wait(sy, v_w1q[fc - 4])
                    C["w1s%d" % (fc % 4)].inc(sy.dma_start(
                        out=w1r[:, (fc % 4) * D:(fc % 4 + 1) * D],
                        in_=w1_in[fc * 128:(fc + 1) * 128, :]), 16)

            for a in range(NA):
                for q in range(NA):
                    emit_AT(tt, a, q)
                if a >= 1:
                    emit_OP(tt, a - 1)
            emit_OP(tt, NA - 1)

            @blk.scalar
            def _(sc, tt=tt):
                C["stao"].wait(sc, 16 * 8 * (tt + 1))
                if tt > 0:
                    V.wait(sc, v_ln2ap[tt * 4 - 1])
                    A.wait(sc, A.n)
                for tc in range(4):
                    ti = tt * 4 + tc
                    C["trao"].inc(sc.dma_start_transpose(
                        aotm[:, tc * D:(tc + 1) * D],
                        ao_d[0:D, ti * 128:(ti + 1) * 128]), 16)

            @blk.vector
            def _(ve, tt=tt):
                C["trao"].wait(ve, 16 * 4 * (tt + 1))
                for tc in range(4):
                    ti = tt * 4 + tc
                    ins = ve.reduce_sum(ssum[:, 16 + ti:17 + ti],
                                        aotm[:, tc * D:(tc + 1) * D], axis=AX.X)
                V.inc(ins, 1)
            v_ln2sum = V.n

            @blk.vector
            def _(ve, tt=tt):
                C["trao"].wait(ve, 16 * 4 * (tt + 1))
                for tc in range(4):
                    ti = tt * 4 + tc
                    src_ = aotm[:, tc * D:(tc + 1) * D]
                    ve.tensor_tensor(out=sqs[:, 0:1024], in0=src_, in1=src_,
                                     op=OP.mult)
                    ve.reduce_sum(ssum[:, 16 + ti:17 + ti], src_, axis=AX.X)
                    ins = ve.reduce_sum(ssq[:, 16 + ti:17 + ti], sqs[:, 0:1024],
                                        axis=AX.X)
                V.inc(ins, 1)
            v_sq2 = [V.n]

            v_ln2fin = ln_finalize(16 + tt * 4, 4)

            for tc in range(4):
                ti = tt * 4 + tc

                @blk.vector
                def _(ve, tt=tt, tc=tc, ti=ti):
                    if tc == 0:
                        V.wait(ve, v_ln2fin)
                        ve.drain()
                    xt_ = xres[:, ti * D:(ti + 1) * D]
                    ve.tensor_scalar(out=scr[:, 1024:2048],
                                     in0=aotm[:, tc * D:(tc + 1) * D],
                                     scalar1=stm[:, 16 + ti:17 + ti],
                                     scalar2=srr[:, 16 + ti:17 + ti],
                                     op0=OP.subtract, op1=OP.mult)
                    ve.tensor_tensor(out=xt_, in0=xt_, in1=scr[:, 1024:2048],
                                     op=OP.add)
                    ve.tensor_tensor(out=sqs[:, 0:1024], in0=xt_, in1=xt_,
                                     op=OP.mult)
                    ve.tensor_copy(ssum[:, 32 + ti:33 + ti], ssum[:, ti:ti + 1])
                    ve.reduce_max(rx[:, ti:ti + 1], xt_, axis=AX.X)
                    V.inc(ve.reduce_sum(ssq[:, 32 + ti:33 + ti], sqs[:, 0:1024],
                                        axis=AX.X), 1)
                v_ln2ap[ti] = V.n

                @blk.gpsimd
                def _(gp, ti=ti):
                    V.wait(gp, v_ln2ap[ti])
                    C["stx1"].inc(gp.dma_start(out=x1_d[ti * 128:(ti + 1) * 128, :],
                                               in_=xres[:, ti * D:(ti + 1) * D]), 16)

            def emit_w1q(fc0, n):
                @blk.vector
                def _(ve):
                    if fc0 == 0:
                        T.wait(ve, t_btr)
                    for i, fc in enumerate(range(fc0, fc0 + n)):
                        C["w1s%d" % (fc % 4)].wait(ve, 16 * (fc // 4 + 1))
                        ve.reduce_max(tny[:, i:i + 1],
                                      w1r[:, (fc % 4) * D:(fc % 4 + 1) * D],
                                      axis=AX.X, apply_absolute_value=True)
                    ve.drain()
                    ve.tensor_scalar(out=sw1m[:, fc0:fc0 + n], in0=tny[:, 0:n],
                                     scalar1=EPS_Q, scalar2=None, op0=OP.max)
                    ve.drain()
                    ve.reciprocal(tny[:, 4:4 + n], sw1m[:, fc0:fc0 + n])
                    ve.drain()
                    ve.tensor_scalar(out=tny[:, 0:n], in0=tny[:, 4:4 + n],
                                     scalar1=127.0, scalar2=None, op0=OP.mult)
                    ve.drain()
                    for i, fc in enumerate(range(fc0, fc0 + n)):
                        if fc >= 2:
                            C["stj1"].wait(ve, 16 * fc)
                        ve.tensor_scalar(out=sci[:, (i % 2) * 1024:(i % 2 + 1) * 1024],
                                         in0=w1r[:, (fc % 4) * D:(fc % 4 + 1) * D],
                                         scalar1=tny[:, i:i + 1], scalar2=None,
                                         op0=OP.mult)
                        V.inc(ve.tensor_copy(j1r[:, (fc % 2) * D:(fc % 2 + 1) * D],
                                             sci[:, (i % 2) * 1024:(i % 2 + 1) * 1024]), 1)
                        v_w1q[fc] = V.n

                @blk.gpsimd
                def _(gp):
                    for fc in range(fc0, fc0 + n):
                        V.wait(gp, v_w1q[fc])
                        C["stj1"].inc(gp.dma_start(
                            out=j1_d[fc * 128:(fc + 1) * 128, :],
                            in_=j1r[:, (fc % 2) * D:(fc % 2 + 1) * D]), 16)

            emit_w1q(tt * 8, 4)

            @blk.sync
            def _(sy, tt=tt):
                for fc in range(tt * 8 + 4, tt * 8 + 8):
                    V.wait(sy, v_w1q[fc - 4])
                    C["w1s%d" % (fc % 4)].inc(sy.dma_start(
                        out=w1r[:, (fc % 4) * D:(fc % 4 + 1) * D],
                        in_=w1_in[fc * 128:(fc + 1) * 128, :]), 16)

            emit_w1q(tt * 8 + 4, 4)

        a_ln3sq = A.n

        @blk.vector
        def _(ve):
            A.wait(ve, a_ln3sq)
        v_ln3fin = ln_finalize(32, 16)

        attn.close()
        p2.close()
        ph_x.close()

        phF = ExitStack()
        pbc = phF.enter_context(nc.psum_tensor("pbc", [128, 16], F32))
        pcc = phF.enter_context(nc.psum_tensor("pcc", [1, 128], F32))
        x1rd = phF.enter_context(nc.sbuf_tensor("x1rd", [128, 2 * D], F32))

        @blk.vector
        def _(ve):
            V.wait(ve, v_ln3fin)
            ve.drain()
            ve.tensor_tensor(out=sA[:, 0:16], in0=rx[:, 0:16], in1=stm[:, 32:48],
                             op=OP.subtract)
            ve.drain()
            ve.tensor_tensor(out=sB[:, 0:16], in0=sA[:, 0:16], in1=srr[:, 32:48],
                             op=OP.mult)
            ve.drain()
            V.inc(ve.reduce_max(tny[:, 3:4], sB[:, 0:16], axis=AX.X), 1)
        v_lmax1 = V.n

        @blk.tensor
        def _(te):
            V.wait(te, v_lmax1)
            T.inc(te.transpose(pcc[0:1, :], tny[:, 3:4], ident[:]), 1)
        t_max1 = T.n

        @blk.vector
        def _(ve):
            T.wait(ve, t_max1)
            ve.reduce_max(gsc[0:1, 1:2], pcc[0:1, :], axis=AX.X)
            ve.drain()
            V.inc(ve.tensor_scalar(out=gsc[0:1, 0:1], in0=gsc[0:1, 1:2],
                                   scalar1=EPS_Q, scalar2=None, op0=OP.max), 1)
        v_g1 = V.n

        @blk.gpsimd
        def _(gp):
            V.wait(gp, v_g1)
            C["stx1"].wait(gp, 16 * NT)
            C["stj1"].wait(gp, 16 * 32)
            C["ccd"].inc(gp.dma_start(out=cc1_i[:], in_=gsc[:]), 16)
            C["ccd"].wait(gp)
            C["cc"].inc(gp.collective_compute(
                "AllReduce", OP.max, replica_groups=RG,
                ins=[cc1_i[:]], outs=[cc1_o[:]]), 1)
            C["cc"].wait(gp)
            C["ccd"].inc(gp.dma_start(out=gsc[0:1, 2:3], in_=cc1_o[0:1, 0:1]), 16)
            G.inc(gp.nop(), 1)
        g_cc1 = G.n

        @blk.tensor
        def _(te):
            C["ccd"].wait(te)
            G.wait(te, g_cc1)
            T.inc(te.matmul(pbc[:, 0:1], ones[:], gsc[0:1, 2:3],
                            start=True, stop=True), 1)
        t_bc1 = T.n

        @blk.vector
        def _(ve):
            T.wait(ve, t_bc1)
            V.wait(ve, v_bc)
            ve.tensor_copy(gb[:, 0:1], pbc[:, 0:1])
            ve.drain()
            ve.reciprocal(gb[:, 1:2], gb[:, 0:1])
            ve.drain()
            ve.tensor_scalar(out=gb[:, 2:3], in0=gb[:, 1:2], scalar1=255.0,
                             scalar2=None, op0=OP.mult)
            ve.tensor_scalar(out=gb[:, 3:4], in0=gb[:, 0:1], scalar1=1.0 / 255.0,
                             scalar2=None, op0=OP.mult)
            ve.drain()
            ve.tensor_scalar(out=sb1[:], in0=sw1m[:], scalar1=gb[:, 3:4],
                             scalar2=1.0 / 127.0, op0=OP.mult, op1=OP.mult)
            ve.drain()
            ve.reciprocal(sA[:, 0:32], sb1[:])
            ve.drain()
            ve.tensor_tensor(out=sB[:, 0:32], in0=b1c[:], in1=sA[:, 0:32],
                             op=OP.mult)
            ve.drain()
            ve.tensor_copy(sci[:, 0:32], sB[:, 0:32])
            ve.drain()
            ve.tensor_copy(bq1[:], sci[:, 0:32])
            ve.drain()
            ve.tensor_tensor(out=eb1[:], in0=bq1[:], in1=sb1[:], op=OP.mult)
            ve.tensor_scalar(out=sB[:, 0:16], in0=srr[:, 32:48], scalar1=gb[:, 2:3],
                             scalar2=None, op0=OP.mult)
            ve.drain()
            ve.tensor_tensor(out=sDs[:, 0:16], in0=stm[:, 32:48], in1=sB[:, 0:16],
                             op=OP.mult)
            ve.drain()
            V.inc(ve.tensor_scalar(out=sC[:, 0:16], in0=sDs[:, 0:16], scalar1=-1.0,
                                   scalar2=None, op0=OP.mult), 1)
        v_p4 = V.n

        phK = ExitStack()
        k1T = phK.enter_context(nc.sbuf_tensor("k1T", [128, 8 * S], BF16))
        k1r = phK.enter_context(nc.sbuf_tensor("k1r", [128, 2 * D], F32))
        if DEBUG_OUTPUTS:
            k1rb = phK.enter_context(nc.sbuf_tensor("k1rb", [128, 2 * D], BF16))

        phP = ExitStack()
        pk = phP.enter_context(nc.psum_tensor("pk", [128, 2048], F32))

        a_k1v = []
        v_k1 = []
        t_k1t = []
        a_k1e = []
        v_k1e = []
        for ti in range(NT):
            @blk.sync
            def _(sy, ti=ti):
                C["stx1"].wait(sy, 16 * NT)
                if ti == 0:
                    C["cc"].wait(sy, 1)
                if ti >= 2:
                    A.wait(sy, a_k1v[ti - 2])
                C["x1p%d" % (ti % 2)].inc(
                    sy.dma_start(out=x1rd[:, (ti % 2) * D:(ti % 2 + 1) * D],
                                 in_=x1_d[ti * 128:(ti + 1) * 128, :]), 16)

            @blk.scalar
            def _(sc, ti=ti):
                if ti == 0:
                    V.wait(sc, v_p4)
                C["x1p%d" % (ti % 2)].wait(sc, 16 * (ti // 2 + 1))
                if ti >= 2:
                    V.wait(sc, v_k1[ti - 2])
                A.inc(sc.activation(out=scr[:, (ti % 2) * 1024:(ti % 2) * 1024 + 1024],
                                    in_=x1rd[:, (ti % 2) * D:(ti % 2 + 1) * D],
                                    func=AF.Relu, bias=sC[:, ti:ti + 1],
                                    scale=sB[:, ti:ti + 1]), 1)
            a_k1v.append(A.n)

            @blk.vector
            def _(ve, ti=ti):
                A.wait(ve, a_k1v[ti])
                if ti >= 2:
                    T.wait(ve, t_k1t[ti - 2])
                    if DEBUG_OUTPUTS:
                        C["stk1"].wait(ve, 16 * ti)
                ve.tensor_copy(sci[:, (ti % 2) * 1024:(ti % 2) * 1024 + 1024],
                               scr[:, (ti % 2) * 1024:(ti % 2) * 1024 + 1024])
                V.inc(ve.tensor_copy(k1r[:, (ti % 2) * D:(ti % 2 + 1) * D],
                                     sci[:, (ti % 2) * 1024:(ti % 2) * 1024 + 1024]), 1)
            v_k1.append(V.n)

            if DEBUG_OUTPUTS:
                @blk.vector
                def _(ve, ti=ti):
                    V.inc(ve.tensor_copy(k1rb[:, (ti % 2) * D:(ti % 2 + 1) * D],
                                         k1r[:, (ti % 2) * D:(ti % 2 + 1) * D]), 1)
                v_k1[-1] = V.n

                @blk.gpsimd
                def _(gp, ti=ti):
                    V.wait(gp, v_k1[ti])
                    C["stk1"].inc(gp.dma_start(out=k1_d[ti * 128:(ti + 1) * 128, :],
                                               in_=k1rb[:, (ti % 2) * D:(ti % 2 + 1) * D]), 16)

            @blk.tensor
            def _(te, ti=ti):
                V.wait(te, v_k1[ti])
                if ti >= 2:
                    A.wait(te, a_k1e[ti - 2])
                    V.wait(te, v_k1e[ti - 2])
                for dc in range(8):
                    ins = te.transpose(
                        pk[:, (ti % 2) * 1024 + dc * 128:(ti % 2) * 1024 + dc * 128 + 128],
                        k1r[:, (ti % 2) * D + dc * 128:(ti % 2) * D + dc * 128 + 128],
                        ident[:])
                T.inc(ins, 1)
            t_k1t.append(T.n)

            @blk.scalar
            def _(sc, ti=ti):
                T.wait(sc, t_k1t[ti])
                for dc in range(4):
                    ins = sc.activation(
                        out=k1T[:, dc * S + ti * 128:dc * S + ti * 128 + 128],
                        in_=pk[:, (ti % 2) * 1024 + dc * 128:(ti % 2) * 1024 + dc * 128 + 128],
                        func=AF.Copy, bias=0.0, scale=1.0)
                A.inc(ins, 1)
            a_k1e.append(A.n)

            @blk.vector
            def _(ve, ti=ti):
                T.wait(ve, t_k1t[ti])
                for dc in range(4, 8):
                    ins = ve.tensor_copy(
                        k1T[:, dc * S + ti * 128:dc * S + ti * 128 + 128],
                        pk[:, (ti % 2) * 1024 + dc * 128:(ti % 2) * 1024 + dc * 128 + 128])
                V.inc(ins, 1)
            v_k1e.append(V.n)

        phP.close()

        ph6 = ExitStack()
        j1T = ph6.enter_context(nc.sbuf_tensor("j1T", [128, 8 * FF], BF16))
        w2r = ph6.enter_context(nc.sbuf_tensor("w2r", [128, 2 * FF], F32))
        j2r = ph6.enter_context(nc.sbuf_tensor("j2r", [128, 2 * FF], BF16))
        hr = ph6.enter_context(nc.sbuf_tensor("hr", [128, 6 * 512], F16))
        p6 = ph6.enter_context(nc.psum_tensor("p6", [128, 3072], F32))

        @blk.sync
        def _(sy):
            C["stj1"].wait(sy, 16 * 32)
            C["cc"].wait(sy, 1)
            for dc in range(8):
                C["trj1"].inc(sy.dma_start_transpose(
                    j1T[:, dc * FF:(dc + 1) * FF],
                    j1_d[0:FF, dc * 128:(dc + 1) * 128]), 16)

        v_w2q = [0] * 8
        t_h = []
        b_free = []
        v_h = []
        v_hm = []
        idx = 0
        for fc in range(32):
            for tt in range(TT):
                bank = idx % 6

                @blk.tensor
                def _(te, fc=fc, tt=tt, bank=bank, idx=idx):
                    if idx == 0:
                        A.wait(te, a_k1e[-1])
                        V.wait(te, v_k1e[-1])
                        C["trj1"].wait(te, 16 * 8)
                    if idx >= 6:
                        V.wait(te, b_free[idx - 6])
                    for dc in range(8):
                        ins = te.matmul(
                            p6[:, bank * 512:(bank + 1) * 512],
                            j1T[:, dc * FF + fc * 128:dc * FF + fc * 128 + 128],
                            k1T[:, dc * S + tt * 512:dc * S + tt * 512 + 512],
                            start=(dc == 0), stop=(dc == 7))
                    T.inc(ins, 1)
                t_h.append(T.n)

                @blk.vector
                def _(ve, fc=fc, bank=bank, idx=idx):
                    T.wait(ve, t_h[idx])
                    if idx >= 6:
                        C["sth"].wait(ve, 16 * idx)
                    V.inc(ve.tensor_scalar(out=scr[:, 0:512],
                                           in0=p6[:, bank * 512:(bank + 1) * 512],
                                           scalar1=sb1[:, fc:fc + 1],
                                           scalar2=eb1[:, fc:fc + 1],
                                           op0=OP.mult, op1=OP.add), 1)
                    b_free.append(V.n)
                    V.inc(ve.tensor_scalar(out=hr[:, (idx % 6) * 512:(idx % 6 + 1) * 512],
                                           in0=scr[:, 0:512], scalar1=0.0,
                                           scalar2=None, op0=OP.max), 1)
                    v_h.append(V.n)
                    V.inc(ve.reduce_max(hm[:, idx:idx + 1],
                                        hr[:, (idx % 6) * 512:(idx % 6 + 1) * 512],
                                        axis=AX.X), 1)
                    v_hm.append(V.n)

                @blk.sync
                def _(sy, fc=fc, tt=tt, idx=idx):
                    V.wait(sy, v_h[idx])
                    C["sth"].inc(sy.dma_start(
                        out=h_d[fc * 128:(fc + 1) * 128, tt * 512:(tt + 1) * 512],
                        in_=hr[:, (idx % 6) * 512:(idx % 6 + 1) * 512]), 16)
                idx += 1

        for oc in range(8):

            @blk.sync
            def _(sy, oc=oc):
                if oc >= 2:
                    V.wait(sy, v_w2q[oc - 2])
                C["w2s%d" % (oc % 2)].inc(sy.dma_start(
                    out=w2r[:, (oc % 2) * FF:(oc % 2 + 1) * FF],
                    in_=w2_in[oc * 128:(oc + 1) * 128, :]), 16)

            @blk.vector
            def _(ve, oc=oc):
                C["w2s%d" % (oc % 2)].wait(ve, 16 * (oc // 2 + 1))
                if oc >= 2:
                    C["stj2"].wait(ve, 16 * oc)
                wsl = w2r[:, (oc % 2) * FF:(oc % 2 + 1) * FF]
                ve.reduce_max(tny[:, 4:5], wsl, axis=AX.X,
                              apply_absolute_value=True)
                ve.drain()
                ve.tensor_scalar(out=sw2m[:, oc:oc + 1], in0=tny[:, 4:5],
                                 scalar1=EPS_Q, scalar2=None, op0=OP.max)
                ve.drain()
                ve.reciprocal(tny[:, 5:6], sw2m[:, oc:oc + 1])
                ve.drain()
                ve.tensor_scalar(out=tny[:, 6:7], in0=tny[:, 5:6],
                                 scalar1=127.0, scalar2=None, op0=OP.mult)
                ve.drain()
                for half in range(2):
                    ve.tensor_scalar(out=sci[:, (half % 2) * 2048:
                                         (half % 2) * 2048 + 2048][:, 0:2048]
                                     if False else sci[:, 0:2048],
                                     in0=wsl[:, half * 2048:(half + 1) * 2048],
                                     scalar1=tny[:, 6:7], scalar2=None,
                                     op0=OP.mult)
                    ins = ve.tensor_copy(
                        j2r[:, (oc % 2) * FF + half * 2048:
                            (oc % 2) * FF + half * 2048 + 2048],
                        sci[:, 0:2048])
                V.inc(ins, 1)
                v_w2q[oc] = V.n

            @blk.gpsimd
            def _(gp, oc=oc):
                V.wait(gp, v_w2q[oc])
                C["stj2"].inc(gp.dma_start(
                    out=j2_d[oc * 128:(oc + 1) * 128, :],
                    in_=j2r[:, (oc % 2) * FF:(oc % 2 + 1) * FF]), 16)

        ph6.close()
        phK.close()

        @blk.vector
        def _(ve):
            V.wait(ve, v_hm[-1])
            ve.drain()
            V.inc(ve.reduce_max(tny[:, 7:8], hm[:, 0:128], axis=AX.X), 1)
        v_lmax2 = V.n

        @blk.tensor
        def _(te):
            V.wait(te, v_lmax2)
            T.inc(te.transpose(pcc[0:1, :], tny[:, 7:8], ident[:]), 1)
        t_max2 = T.n

        @blk.vector
        def _(ve):
            T.wait(ve, t_max2)
            ve.reduce_max(gsc[0:1, 1:2], pcc[0:1, :], axis=AX.X)
            ve.drain()
            V.inc(ve.tensor_scalar(out=gsc[0:1, 0:1], in0=gsc[0:1, 1:2],
                                   scalar1=EPS_Q, scalar2=None, op0=OP.max), 1)
        v_g2 = V.n

        @blk.gpsimd
        def _(gp):
            V.wait(gp, v_g2)
            C["sth"].wait(gp, 16 * 128)
            C["stj2"].wait(gp, 16 * 8)
            C["ccd"].inc(gp.dma_start(out=cc2_i[:], in_=gsc[:]), 16)
            C["ccd"].wait(gp)
            C["cc"].inc(gp.collective_compute(
                "AllReduce", OP.max, replica_groups=RG,
                ins=[cc2_i[:]], outs=[cc2_o[:]]), 1)
            C["cc"].wait(gp)
            C["ccd"].inc(gp.dma_start(out=gsc[0:1, 2:3], in_=cc2_o[0:1, 0:1]), 16)
            G.inc(gp.nop(), 1)
        g_cc2 = G.n

        @blk.tensor
        def _(te):
            C["ccd"].wait(te)
            G.wait(te, g_cc2)
            T.inc(te.matmul(pbc[:, 1:2], ones[:], gsc[0:1, 2:3],
                            start=True, stop=True), 1)
        t_bc2 = T.n


        ph7 = ExitStack()
        j2T = ph7.enter_context(nc.sbuf_tensor("j2T", [128, 32 * 1024], BF16))
        k2b = ph7.enter_context(nc.sbuf_tensor("k2b", [128, 2 * 32 * 512], BF16))
        h2r = ph7.enter_context(nc.sbuf_tensor("h2r", [128, 2 * 2048], F16))
        o2tm = ph7.enter_context(nc.sbuf_tensor("o2tm", [128, 4 * D], BF16))
        x1r4 = ph7.enter_context(nc.sbuf_tensor("x1r4", [128, 4 * D], F32))
        outr = ph7.enter_context(nc.sbuf_tensor("outr", [128, 2 * D], F32))
        o2r = ph7.enter_context(nc.sbuf_tensor("o2r", [128, 2 * 512], BF16))
        p7 = ph7.enter_context(nc.psum_tensor("p7", [128, 2048], F32))

        @blk.vector
        def _(ve):
            T.wait(ve, t_bc2)
            ve.tensor_copy(gb[:, 4:5], pbc[:, 1:2])
            ve.drain()
            ve.reciprocal(gb[:, 5:6], gb[:, 4:5])
            ve.drain()
            ve.tensor_scalar(out=gb[:, 6:7], in0=gb[:, 5:6], scalar1=255.0,
                             scalar2=None, op0=OP.mult)
            V.inc(ve.tensor_scalar(out=gb[:, 7:8], in0=gb[:, 4:5],
                                   scalar1=1.0 / 255.0,
                                   scalar2=None, op0=OP.mult), 1)
        v_p7a = V.n

        @blk.vector
        def _(ve):
            V.wait(ve, v_p7a)
            ve.drain()
            ve.tensor_scalar(out=sb2[:], in0=sw2m[:], scalar1=gb[:, 7:8],
                             scalar2=1.0 / 127.0, op0=OP.mult, op1=OP.mult)
            ve.drain()
            ve.reciprocal(sA[:, 0:8], sb2[:])
            ve.drain()
            ve.tensor_tensor(out=sB[:, 0:8], in0=b2c[:], in1=sA[:, 0:8], op=OP.mult)
            ve.drain()
            ve.tensor_copy(sci[:, 0:8], sB[:, 0:8])
            ve.drain()
            ve.tensor_copy(bq2[:], sci[:, 0:8])
            ve.drain()
            V.inc(ve.tensor_tensor(out=eb2[:], in0=bq2[:], in1=sb2[:], op=OP.mult), 1)
        v_p7 = V.n


        @blk.sync
        def _(sy):
            C["stj2"].wait(sy, 16 * 8)
            C["cc"].wait(sy, 2)
            for fc in range(32):
                C["trj2"].inc(sy.dma_start_transpose(
                    j2T[:, fc * 1024:(fc + 1) * 1024],
                    j2_d[0:D, fc * 128:(fc + 1) * 128]), 16)

        def emit_P8(tt):
                @blk.scalar
                def _(sc, tt=tt):
                    C["sto2"].wait(sc, 16 * 8 * (tt + 1))
                    if tt > 0:
                        V.wait(sc, v_outc[-1])
                        A.wait(sc, A.n)
                    for tc in range(4):
                        ti = tt * 4 + tc
                        C["tro2"].inc(sc.dma_start_transpose(
                            o2tm[:, tc * D:(tc + 1) * D],
                            o2_d[0:D, ti * 128:(ti + 1) * 128]), 16)
                        C["x1p%d" % (ti % 2)].inc(sc.dma_start(
                            out=x1r4[:, tc * D:(tc + 1) * D],
                            in_=x1_d[ti * 128:(ti + 1) * 128, :]), 16)

                @blk.vector
                def _(ve, tt=tt):
                    C["tro2"].wait(ve, 16 * 4 * (tt + 1))
                    for tc in range(4):
                        ins = ve.reduce_sum(ssum[:, 48 + tt * 4 + tc:49 + tt * 4 + tc],
                                            o2tm[:, tc * D:(tc + 1) * D], axis=AX.X)
                    V.inc(ins, 1)
                v_ln4sum = V.n

                @blk.vector
                def _(ve, tt=tt):
                    C["tro2"].wait(ve, 16 * 4 * (tt + 1))
                    for tc in range(4):
                        ti = tt * 4 + tc
                        src_ = o2tm[:, tc * D:(tc + 1) * D]
                        ve.tensor_tensor(out=scr[:, 0:1024], in0=src_, in1=src_,
                                         op=OP.mult)
                        ve.reduce_sum(ssum[:, 48 + ti:49 + ti], src_, axis=AX.X)
                        ins = ve.reduce_sum(ssq[:, 48 + ti:49 + ti], scr[:, 0:1024],
                                            axis=AX.X)
                    V.inc(ins, 1)

                v_ln4fin = ln_finalize(48 + tt * 4, 4)

                for tc in range(4):
                    ti = tt * 4 + tc

                    @blk.vector
                    def _(ve, tt=tt, tc=tc, ti=ti):
                        if tc == 0:
                            V.wait(ve, v_ln4fin)
                            ve.drain()
                        C["x1p%d" % (ti % 2)].wait(ve, 16 * (8 + ti // 2 + 1))
                        if len(v_outc) >= 2:
                            C["stout"].wait(ve, 16 * len(v_outc))
                        ve.tensor_scalar(out=scr[:, 1024:2048],
                                         in0=o2tm[:, tc * D:(tc + 1) * D],
                                         scalar1=stm[:, 48 + ti:49 + ti],
                                         scalar2=srr[:, 48 + ti:49 + ti],
                                         op0=OP.subtract, op1=OP.mult)
                        V.inc(ve.tensor_tensor(out=outr[:, (tc % 2) * D:(tc % 2 + 1) * D],
                                               in0=x1r4[:, tc * D:(tc + 1) * D],
                                               in1=scr[:, 1024:2048], op=OP.add), 1)
                    v_outc.append(V.n)

                    @blk.gpsimd
                    def _(gp, ti=ti, tc=tc):
                        V.wait(gp, v_outc[-1])
                        C["stout"].inc(gp.dma_start(
                            out=y_out[ti * 128:(ti + 1) * 128, :],
                            in_=outr[:, (tc % 2) * D:(tc % 2 + 1) * D]), 16)



        t_o2 = {}
        a_o2 = {}
        v_k2g = {}
        v_outc = []
        for tt in range(TT):
            for g in range(8):
                gi = tt * 8 + g

                @blk.gpsimd
                def _(gp, tt=tt, g=g, gi=gi):
                    C["sth"].wait(gp, 16 * 128)
                    if gi == 0:
                        C["cc"].wait(gp, 2)
                    if gi >= 2:
                        V.wait(gp, v_k2g[gi - 2])
                    C["ldhp%d" % (gi % 2)].inc(gp.dma_start(
                        out=h2r[:, (gi % 2) * 2048:(gi % 2 + 1) * 2048],
                        in_=h_d[g * 512:(g + 1) * 512,
                                tt * 512:(tt + 1) * 512].rearrange(
                                    "(c p) t -> p c t", p=128)), 16)

                @blk.vector
                def _(ve, tt=tt, g=g, gi=gi):
                    if gi == 0:
                        V.wait(ve, v_p7)
                    C["ldhp%d" % (gi % 2)].wait(ve, 16 * (gi // 2 + 1))
                    if tt >= 2 and g == 0:
                        T.wait(ve, t_o2[(tt - 2, 7)])
                    ve.tensor_scalar(out=sci[:, 0:2048],
                                     in0=h2r[:, (gi % 2) * 2048:(gi % 2 + 1) * 2048],
                                     scalar1=gb[:, 6:7], scalar2=None, op0=OP.mult)
                    ve.drain()
                    ins = ve.tensor_copy(
                        k2b[:, (tt % 2) * 16384 + g * 2048:
                            (tt % 2) * 16384 + (g + 1) * 2048],
                        sci[:, 0:2048])
                    ve.drain()
                    V.inc(ins, 1)
                v_k2g[gi] = V.n

            for do in range(8):
                o2i = tt * 8 + do
                bank = o2i % 4

                @blk.tensor
                def _(te, tt=tt, do=do, bank=bank, o2i=o2i):
                    if tt == 0 and do == 0:
                        V.wait(te, b_free[-1])
                    if do == 0:
                        V.wait(te, v_k2g[tt * 8 + 7])
                        C["trj2"].wait(te, 16 * 32)
                    if o2i >= 4:
                        V.wait(te, a_o2[o2i - 4])
                    for fc in range(32):
                        ins = te.matmul(
                            p7[:, bank * 512:(bank + 1) * 512],
                            j2T[:, fc * 1024 + do * 128:fc * 1024 + do * 128 + 128],
                            k2b[:, (tt % 2) * 16384 + fc * 512:
                                (tt % 2) * 16384 + fc * 512 + 512],
                            start=(fc == 0), stop=(fc == 31))
                    T.inc(ins, 1)
                t_o2[(tt, do)] = T.n

                @blk.vector
                def _(ve, tt=tt, do=do, bank=bank, o2i=o2i):
                    T.wait(ve, t_o2[(tt, do)])
                    if o2i >= 2:
                        C["sto2"].wait(ve, 16 * o2i)
                    V.inc(ve.tensor_scalar(
                        out=o2r[:, (do % 2) * 512:(do % 2 + 1) * 512],
                        in0=p7[:, bank * 512:(bank + 1) * 512],
                        scalar1=bq2[:, do:do + 1], scalar2=sb2[:, do:do + 1],
                        op0=OP.add, op1=OP.mult), 1)
                a_o2[o2i] = V.n

                @blk.sync
                def _(sy, tt=tt, do=do):
                    V.wait(sy, a_o2[tt * 8 + do])
                    C["sto2"].inc(sy.dma_start(
                        out=o2_d[do * 128:(do + 1) * 128, tt * 512:(tt + 1) * 512],
                        in_=o2r[:, (do % 2) * 512:(do % 2 + 1) * 512]), 16)

            if tt >= 1:
                emit_P8(tt - 1)

        emit_P8(TT - 1)

        @blk.gpsimd
        def _(gp):
            C["stout"].wait(gp, 16 * NT)
            C["stout"].inc(gp.dma_start(out=dbg_d[:, 0:64], in_=ssum[:]), 16)
            C["stout"].inc(gp.dma_start(out=dbg_d[:, 64:128], in_=ssq[:]), 16)
            C["stout"].inc(gp.dma_start(out=dbg_d[:, 128:192], in_=stm[:]), 16)
            C["stout"].inc(gp.dma_start(out=dbg_d[:, 192:256], in_=srr[:]), 16)
            if DEBUG_OUTPUTS:
                C["stout"].inc(gp.dma_start(out=dbg2_d[:, 192:200], in_=sw2m[:]), 16)
                C["stout"].inc(gp.dma_start(out=dbg2_d[:, 200:208], in_=sb2[:]), 16)
                C["stout"].inc(gp.dma_start(out=dbg2_d[:, 208:216], in_=gb[:]), 16)
                C["stout"].wait(gp, 16 * NT + 64 + 48)
            C["stout"].wait(gp, 16 * NT + 64)

        ph7.close()
        phF.close()

        for name, ctr in C.items():
            assert ctr.max_waited <= ctr.n, (name, ctr.max_waited, ctr.n)

    return nc


_NC_CACHE = {}


def kernel(L0_attn_hidden_states, L0_attn_queries, L0_attn_combiners,
           L0_ffn_w1, L0_ffn_b1, L0_ffn_w2, L0_ffn_b2):
    x = np.ascontiguousarray(np.asarray(L0_attn_hidden_states, dtype=np.float32))
    qw = np.ascontiguousarray(np.asarray(L0_attn_queries, dtype=np.float32))
    cb = np.ascontiguousarray(np.asarray(L0_attn_combiners, dtype=np.float32))
    w1 = np.ascontiguousarray(np.asarray(L0_ffn_w1, dtype=np.float32))
    b1 = np.ascontiguousarray(np.asarray(L0_ffn_b1, dtype=np.float32))
    w2 = np.ascontiguousarray(np.asarray(L0_ffn_w2, dtype=np.float32))
    b2 = np.ascontiguousarray(np.asarray(L0_ffn_b2, dtype=np.float32))

    if "nc" not in _NC_CACHE:
        _NC_CACHE["nc"] = build_nc()
    nc = _NC_CACHE["nc"]

    qwT = np.ascontiguousarray(qw.transpose(0, 2, 1))
    in_maps = [{"x0": x[b], "qwT": qwT, "cb": cb, "w1": w1, "b1": b1,
                "w2": w2, "b2": b2} for b in range(8)]
    res = run_bass_kernel_spmd(nc, in_maps, core_ids=list(range(8)))
    return np.stack([np.asarray(r["y"], dtype=np.float32) for r in res.results])


# version 25
# speedup vs baseline: 1.0525x; 1.0089x over previous
import numpy as np
from contextlib import ExitStack
import concourse.bass as bass
import concourse.mybir as mybir
from concourse.bass_utils import run_bass_kernel_spmd

F32 = mybir.dt.float32
BF16 = mybir.dt.bfloat16
F16 = mybir.dt.float16
I32 = mybir.dt.int32
AF = mybir.ActivationFunctionType
OP = mybir.AluOpType
AX = mybir.AxisListType

S, D, FF = 2048, 1024, 4096
NT, TT = 16, 4
NA, H = 4, 256
EPS_LN, EPS_Q = 1e-5, 1e-8
RG = [[0, 1, 2, 3, 4, 5, 6, 7]]
INV_D = 1.0 / D


class Ctr:
    def __init__(self, sem):
        self.sem = sem
        self.n = 0
        self.max_waited = 0

    def inc(self, ins, k):
        ins.then_inc(self.sem, k)
        self.n += k
        return self.n

    def wait(self, eng, val=None):
        if val is None:
            val = self.n
        if val > 0:
            eng.wait_ge(self.sem, val)
            self.max_waited = max(self.max_waited, val)


DEBUG_OUTPUTS = False


def build_nc():
    nc = bass.Bass("TRN2", target_bir_lowering=False)
    dbg = "ExternalOutput" if DEBUG_OUTPUTS else "Internal"

    x_in = nc.dram_tensor("x0", [S, D], F32, kind="ExternalInput").ap()
    qt_in = nc.dram_tensor("qwT", [NA, D, H], F32, kind="ExternalInput").ap()
    c_in = nc.dram_tensor("cb", [NA, D, H], F32, kind="ExternalInput").ap()
    w1_in = nc.dram_tensor("w1", [FF, D], F32, kind="ExternalInput").ap()
    b1_in = nc.dram_tensor("b1", [FF], F32, kind="ExternalInput").ap()
    w2_in = nc.dram_tensor("w2", [D, FF], F32, kind="ExternalInput").ap()
    b2_in = nc.dram_tensor("b2", [D], F32, kind="ExternalInput").ap()
    y_out = nc.dram_tensor("y", [S, D], F32, kind="ExternalOutput").ap()

    ln1_d = nc.dram_tensor("ln1_d", [S, D], BF16, kind=dbg).ap()
    x1_d = nc.dram_tensor("x1_d", [S, D], F32, kind=dbg).ap()
    k1_d = nc.dram_tensor("k1_d", [S, D], BF16, kind=dbg).ap()
    ao_d = nc.dram_tensor("ao_d", [D, S], BF16, kind=dbg).ap()
    o2_d = nc.dram_tensor("o2_d", [D, S], BF16, kind=dbg).ap()
    j1_d = nc.dram_tensor("j1_d", [FF, D], BF16, kind=dbg).ap()
    j2_d = nc.dram_tensor("j2_d", [D, FF], BF16, kind=dbg).ap()
    h_d = nc.dram_tensor("h_d", [FF, S], F16, kind=dbg).ap()
    dbg_d = nc.dram_tensor("dbg_d", [128, 256], F32, kind=dbg).ap()
    dbg2_d = nc.dram_tensor("dbg2_d", [128, 256], F32, kind=dbg).ap()
    dbg3_d = nc.dram_tensor("dbg3_d", [128, 4096], BF16, kind=dbg).ap()
    cc1_i = nc.dram_tensor("cc1_i", [1, 16], F32, kind="Internal").ap()
    cc1_o = nc.dram_tensor("cc1_o", [1, 16], F32, kind="Internal").ap()
    cc2_i = nc.dram_tensor("cc2_i", [1, 16], F32, kind="Internal").ap()
    cc2_o = nc.dram_tensor("cc2_o", [1, 16], F32, kind="Internal").ap()

    top = ExitStack()
    with top:
        def sbuf(name, shape, dt):
            return top.enter_context(nc.sbuf_tensor(name, shape, dt))

        stm = sbuf("stm", [128, 64], F32)
        srr = sbuf("srr", [128, 64], F32)
        sA = sbuf("sA", [128, 64], F32)
        sB = sbuf("sB", [128, 64], F32)
        sC = sbuf("sC", [128, 64], F32)
        sDs = sbuf("sDs", [128, 64], F32)
        ssum = sbuf("ssum", [128, 64], F32)
        ssq = sbuf("ssq", [128, 64], F32)
        rx = sbuf("rx", [128, 16], F32)
        hm = sbuf("hm", [128, 128], F32)
        sw1m = sbuf("sw1m", [128, 32], F32)
        sb1 = sbuf("sb1", [128, 32], F32)
        eb1 = sbuf("eb1", [128, 32], F32)
        bq1 = sbuf("bq1", [128, 32], F32)
        sw2m = sbuf("sw2m", [128, 8], F32)
        sb2 = sbuf("sb2", [128, 8], F32)
        eb2 = sbuf("eb2", [128, 8], F32)
        bq2 = sbuf("bq2", [128, 8], F32)
        b1c = sbuf("b1c", [128, 32], F32)
        b2c = sbuf("b2c", [128, 8], F32)
        ident = sbuf("ident", [128, 128], F32)
        identb = sbuf("identb", [128, 128], BF16)
        io1 = sbuf("io1", [128, 128], F32)
        io2 = sbuf("io2", [128, 1], F32)
        ones = sbuf("ones", [1, 128], F32)
        gb = sbuf("gb", [128, 8], F32)
        gsc = sbuf("gsc", [1, 16], F32)
        scr = sbuf("scr", [128, 2048], F32)
        sci = sbuf("sci", [128, 2048], I32)
        tny = sbuf("tny", [128, 8], F32)

        C = {}
        for name in ["w1s0", "w1s1", "w1s2", "w1s3", "w1s4", "w1s5", "w1s6", "w1s7",
                     "w2s0", "w2s1", "x1p0", "x1p1", "ldhp0", "ldhp1",
                     "ldx", "ldq", "stln", "trln",
                     "stao", "trao", "stx1", "stk1", "trk1",
                     "stj1", "trj1", "stj2", "trj2", "sth",
                     "sto2", "tro2", "stout", "cc", "ccd",
                     "V", "A", "T", "G"]:
            C[name] = Ctr(top.enter_context(nc.semaphore(name)))
        V, A, T, G = C["V"], C["A"], C["T"], C["G"]

        blk = top.enter_context(nc.Block())

        ph_x = ExitStack()
        xres = ph_x.enter_context(nc.sbuf_tensor("xres", [128, NT * D], F32))
        sqs = ph_x.enter_context(nc.sbuf_tensor("sqs", [128, 2048], F32))
        p01 = ExitStack()
        pb = p01.enter_context(nc.psum_tensor("pb", [128, 512], F32))
        pm = [p01.enter_context(nc.psum_tensor("pm%d" % q, [128, 512], F32))
              for q in range(NA)]

        @blk.sync
        def _(sy):
            for hh in range(2):
                C["ldx"].inc(sy.dma_start(
                    out=xres[:, hh * 8 * D:(hh + 1) * 8 * D],
                    in_=x_in[hh * 1024:(hh + 1) * 1024, :].rearrange(
                        "(t p) d -> p t d", p=128)), 16)

        attn = ExitStack()
        qtb = attn.enter_context(nc.sbuf_tensor("qtb", [128, 32 * H], BF16))
        wpp = attn.enter_context(nc.sbuf_tensor("wpp", [128, 16 * 512], BF16))
        cbf = attn.enter_context(nc.sbuf_tensor("cbf", [128, 32 * H], BF16))
        lnr = attn.enter_context(nc.sbuf_tensor("lnr", [128, 2 * D], BF16))
        Mbf = attn.enter_context(nc.sbuf_tensor("Mbf", [128, 8 * H], BF16))
        ln1T = attn.enter_context(nc.sbuf_tensor("ln1T", [128, 8 * 512], BF16))
        atb = attn.enter_context(nc.sbuf_tensor("atb", [128, 2 * 8 * 512], BF16))
        aob = attn.enter_context(nc.sbuf_tensor("aob", [128, 2 * 512], BF16))
        aotm = attn.enter_context(nc.sbuf_tensor("aotm", [128, 4 * D], BF16))
        w1r = attn.enter_context(nc.sbuf_tensor("w1r", [128, 4 * D], F32))
        j1r = attn.enter_context(nc.sbuf_tensor("j1r", [128, 2 * D], BF16))

        @blk.gpsimd
        def _(gp):
            gp.iota(io1[:], pattern=[[1, 128]], base=0, channel_multiplier=0,
                    allow_small_or_imprecise_dtypes=True)
            G.inc(gp.iota(io2[:], pattern=[[0, 1]], base=0, channel_multiplier=1,
                          allow_small_or_imprecise_dtypes=True), 1)
            for a in range(NA):
                for r in range(8):
                    C["ldq"].inc(gp.dma_start(
                        out=qtb[:, (a * 8 + r) * H:(a * 8 + r + 1) * H],
                        in_=qt_in[a, r * 128:(r + 1) * 128, :]), 16)
            for a in range(NA):
                for kc in range(8):
                    C["ldq"].inc(gp.dma_start(
                        out=cbf[:, (a * 8 + kc) * H:(a * 8 + kc + 1) * H],
                        in_=c_in[a, kc * 128:(kc + 1) * 128, :]), 16)
            C["ldq"].inc(gp.dma_start(
                out=scr[0:32, 0:128],
                in_=b1_in.rearrange("(a b) -> a b", b=128)), 16)
            C["ldq"].inc(gp.dma_start(
                out=scr[0:8, 128:256],
                in_=b2_in.rearrange("(a b) -> a b", b=128)), 16)

        @blk.vector
        def _(ve):
            G.wait(ve)
            ve.tensor_scalar(out=ident[:], in0=io1[:], scalar1=io2[:, 0:1],
                             scalar2=None, op0=OP.is_equal)
            ve.memset(ones[:], 1.0)
            ve.drain()
            ve.tensor_copy(identb[:], ident[:])
            V.inc(ve.memset(gsc[:], 0.0), 1)
        v_ident = V.n

        @blk.tensor
        def _(te):
            C["ldq"].wait(te)
            V.wait(te, v_ident)
            te.transpose(pb[:, 0:32], scr[0:32, 0:128], ident[0:32, 0:32])
            T.inc(te.transpose(pb[:, 32:40], scr[0:8, 128:256], ident[0:8, 0:8]), 1)
        t_btr = T.n

        @blk.vector
        def _(ve):
            T.wait(ve, t_btr)
            ve.tensor_copy(b1c[:], pb[:, 0:32])
            ve.drain()
            V.inc(ve.tensor_copy(b2c[:], pb[:, 32:40]), 1)
        v_bc = V.n

        @blk.vector
        def _(ve):
            for ti in range(NT):
                if ti % 8 == 0:
                    C["ldx"].wait(ve, 16 * (ti // 8 + 1))
                xt_ = xres[:, ti * D:(ti + 1) * D]
                ve.tensor_tensor(out=sqs[:, 0:1024], in0=xt_, in1=xt_, op=OP.mult)
                ve.reduce_sum(ssum[:, ti:ti + 1], xt_, axis=AX.X)
                ins = ve.reduce_sum(ssq[:, ti:ti + 1], sqs[:, 0:1024], axis=AX.X)
            V.inc(ins, 1)
        v_ln1sum = V.n
        a_ln1sq = A.n

        def ln_finalize(lo, n):
            h = n // 2
            sla, slb = slice(lo, lo + h), slice(lo + h, lo + n)
            sl = slice(lo, lo + n)

            @blk.vector
            def _(ve):
                ve.drain()
                ve.tensor_scalar(out=stm[:, sla], in0=ssum[:, sla], scalar1=INV_D,
                                 scalar2=None, op0=OP.mult)
                ve.tensor_scalar(out=stm[:, slb], in0=ssum[:, slb], scalar1=INV_D,
                                 scalar2=None, op0=OP.mult)
                ve.tensor_scalar(out=sA[:, sla], in0=ssq[:, sla], scalar1=INV_D,
                                 scalar2=EPS_LN, op0=OP.mult, op1=OP.add)
                ve.tensor_scalar(out=sA[:, slb], in0=ssq[:, slb], scalar1=INV_D,
                                 scalar2=EPS_LN, op0=OP.mult, op1=OP.add)
                ve.tensor_tensor(out=sB[:, sla], in0=stm[:, sla], in1=stm[:, sla],
                                 op=OP.mult)
                ve.tensor_tensor(out=sB[:, slb], in0=stm[:, slb], in1=stm[:, slb],
                                 op=OP.mult)
                ve.tensor_tensor(out=sDs[:, sla], in0=sA[:, sla], in1=sB[:, sla],
                                 op=OP.subtract)
                V.inc(ve.tensor_tensor(out=sDs[:, slb], in0=sA[:, slb],
                                       in1=sB[:, slb], op=OP.subtract), 1)
            v1 = V.n

            @blk.scalar
            def _(sc):
                V.wait(sc, v1)
                A.inc(sc.activation(out=sB[:, sl], in_=sDs[:, sl], func=AF.Sqrt), 1)
            a1 = A.n

            @blk.vector
            def _(ve):
                A.wait(ve, a1)
                ve.drain()
                ve.reciprocal(sC[:, sla], sB[:, sla])
                ve.reciprocal(sC[:, slb], sB[:, slb])
                ve.tensor_tensor(out=sA[:, sla], in0=sC[:, sla], in1=sC[:, sla],
                                 op=OP.mult)
                ve.tensor_tensor(out=sA[:, slb], in0=sC[:, slb], in1=sC[:, slb],
                                 op=OP.mult)
                ve.tensor_tensor(out=sB[:, sla], in0=sDs[:, sla], in1=sA[:, sla],
                                 op=OP.mult)
                ve.tensor_tensor(out=sB[:, slb], in0=sDs[:, slb], in1=sA[:, slb],
                                 op=OP.mult)
                ve.tensor_scalar(out=sA[:, sla], in0=sB[:, sla], scalar1=-0.5,
                                 scalar2=1.5, op0=OP.mult, op1=OP.add)
                ve.tensor_scalar(out=sA[:, slb], in0=sB[:, slb], scalar1=-0.5,
                                 scalar2=1.5, op0=OP.mult, op1=OP.add)
                ve.tensor_tensor(out=srr[:, sla], in0=sC[:, sla], in1=sA[:, sla],
                                 op=OP.mult)
                V.inc(ve.tensor_tensor(out=srr[:, slb], in0=sC[:, slb],
                                       in1=sA[:, slb], op=OP.mult), 1)
            return V.n

        v_ln1finA = ln_finalize(0, 8)
        v_ln1fin = ln_finalize(8, 8)

        if DEBUG_OUTPUTS:
            @blk.gpsimd
            def _(gp):
                V.wait(gp, v_ln1fin)
                C["ccd"].inc(gp.dma_start(out=dbg2_d[:, 0:64], in_=srr[:]), 16)
                C["ccd"].inc(gp.dma_start(out=dbg2_d[:, 64:128], in_=sC[:]), 16)
                C["ccd"].inc(gp.dma_start(out=dbg2_d[:, 192:256], in_=sDs[:]), 16)
                C["ccd"].wait(gp)

        v_ln1ap = []
        t_gram = []
        for ti in range(NT):
            slot = ti % 2

            @blk.vector
            def _(ve, ti=ti, slot=slot):
                if ti == 0:
                    V.wait(ve, v_ln1finA)
                    ve.drain()
                if ti == 8:
                    V.wait(ve, v_ln1fin)
                    ve.drain()
                if ti >= 2:
                    T.wait(ve, t_gram[ti - 2])
                    C["stln"].wait(ve, 16 * (ti - 1))
                V.inc(ve.tensor_scalar(out=lnr[:, slot * D:(slot + 1) * D],
                                       in0=xres[:, ti * D:(ti + 1) * D],
                                       scalar1=stm[:, ti:ti + 1],
                                       scalar2=srr[:, ti:ti + 1],
                                       op0=OP.subtract, op1=OP.mult), 1)
            v_ln1ap.append(V.n)

            @blk.tensor
            def _(te, ti=ti, slot=slot):
                V.wait(te, v_ln1ap[ti])
                for q in range(NA):
                    base = slot * D + q * H
                    for fc in range(2):
                        ins = te.matmul(pm[q][:, fc * H:(fc + 1) * H],
                                        lnr[:, base + fc * 128:base + fc * 128 + 128],
                                        lnr[:, base:base + H],
                                        start=(ti == 0 and fc == 0),
                                        stop=(ti == NT - 1))
                T.inc(ins, 1)
            t_gram.append(T.n)

            @blk.gpsimd
            def _(gp, ti=ti, slot=slot):
                V.wait(gp, v_ln1ap[ti])
                C["stln"].inc(gp.dma_start(out=ln1_d[ti * 128:(ti + 1) * 128, :],
                                           in_=lnr[:, slot * D:(slot + 1) * D]), 16)

        @blk.vector
        def _(ve):
            T.wait(ve, t_gram[-1])
            for q in range(NA):
                for fc in range(2):
                    ins = ve.tensor_copy(Mbf[:, (q * 2 + fc) * H:(q * 2 + fc + 1) * H],
                                         pm[q][:, fc * H:(fc + 1) * H])
            V.inc(ins, 1)
        v_mevac = V.n

        if DEBUG_OUTPUTS:
            @blk.gpsimd
            def _(gp):
                V.wait(gp, v_mevac)
                C["ccd"].inc(gp.dma_start(out=dbg3_d[:, 0:2048], in_=Mbf[:]), 16)
                C["ccd"].inc(gp.dma_start(out=dbg2_d[:, 128:192], in_=srr[:]), 16)
                C["ccd"].wait(gp)

        p01.close()
        p2 = ExitStack()
        pat = p2.enter_context(nc.psum_tensor("pat", [128, 2048], F32))
        po = p2.enter_context(nc.psum_tensor("po", [128, 2048], F32))

        t_wpp = []
        a_wpp = []
        for i in range(16):
            a, q = i // 4, i % 4
            boff = (i % 2) * 512

            @blk.tensor
            def _(te, a=a, q=q, boff=boff, i=i):
                if i == 0:
                    V.wait(te, v_mevac)
                    C["ldq"].wait(te)
                if i >= 2:
                    A.wait(te, a_wpp[i - 2])
                for ec in range(2):
                    for fc in range(2):
                        ins = te.matmul(
                            pat[:, boff + ec * H:boff + ec * H + H],
                            qtb[:, (a * 8 + q * 2 + fc) * H + ec * 128:
                                (a * 8 + q * 2 + fc) * H + ec * 128 + 128],
                            Mbf[:, (q * 2 + fc) * H:(q * 2 + fc + 1) * H],
                            start=(ec == 0 and fc == 0), stop=(ec == 1 and fc == 1))
                T.inc(ins, 1)
            t_wpp.append(T.n)

            @blk.scalar
            def _(sc, i=i, boff=boff):
                T.wait(sc, t_wpp[i])
                A.inc(sc.copy(out=wpp[:, i * 512:(i + 1) * 512],
                              in_=pat[:, boff:boff + 512]), 1)
            a_wpp.append(A.n)

        t_at = {}
        a_at = {}
        a_at_par = {}
        t_op = {}
        t_op_par = {}
        v_ao = {}
        v_ao_par = {}
        t_lastAT = [0] * TT
        v_w1q = [0] * 32
        v_ln2ap = {}

        def emit_AT(tt, a, q):
            @blk.tensor
            def _(te):
                if a == 0 and q == 0:
                    C["trln"].wait(te, 16 * 8 * (tt + 1))
                    if tt == 0:
                        A.wait(te, a_wpp[-1])
                pk = q % 2
                if pk in a_at_par:
                    eng, cnt = a_at_par[pk]
                    C[eng].wait(te, cnt)
                for gc in range(2):
                    for ec in range(2):
                        ins = te.matmul(
                            pat[:, pk * 1024 + gc * 512:pk * 1024 + gc * 512 + 512],
                            wpp[:, (a * 4 + q) * 512 + ec * H + gc * 128:
                                (a * 4 + q) * 512 + ec * H + gc * 128 + 128],
                            ln1T[:, (a * 2 + ec) * 512:(a * 2 + ec + 1) * 512],
                            start=(ec == 0), stop=(ec == 1))
                T.inc(ins, 1)
            t_at[(tt, a, q)] = T.n
            if a == NA - 1 and q == NA - 1:
                t_lastAT[tt] = T.n

            @blk.scalar
            def _(sc):
                T.wait(sc, t_at[(tt, a, q)])
                ap = a % 2
                if ap in t_op_par:
                    T.wait(sc, t_op_par[ap])
                pk = q % 2
                for gc in range(2):
                    ins = sc.copy(
                        out=atb[:, ap * 4096 + (q * 2 + gc) * 512:
                                ap * 4096 + (q * 2 + gc + 1) * 512],
                        in_=pat[:, pk * 1024 + gc * 512:pk * 1024 + gc * 512 + 512])
                A.inc(ins, 1)
            a_at[(tt, a, q)] = ("A", A.n)
            a_at_par[q % 2] = a_at[(tt, a, q)]

        def emit_OP(tt, a):
            @blk.tensor
            def _(te):
                A.wait(te, a_at[(tt, a, 3)][1])
                ap = a % 2
                if ap in v_ao_par:
                    V.wait(te, v_ao_par[ap])
                for oc in range(2):
                    for kc in range(8):
                        ins = te.matmul(
                            po[:, ap * 1024 + oc * 512:ap * 1024 + oc * 512 + 512],
                            cbf[:, (a * 8 + kc) * H + oc * 128:
                                (a * 8 + kc) * H + oc * 128 + 128],
                            atb[:, ap * 4096 + kc * 512:ap * 4096 + (kc + 1) * 512],
                            start=(kc == 0), stop=(kc == 7))
                T.inc(ins, 1)
            t_op[(tt, a)] = T.n
            t_op_par[a % 2] = T.n

            @blk.vector
            def _(ve):
                T.wait(ve, t_op[(tt, a)])
                n_prev = len(v_ao)
                if n_prev >= 1:
                    C["stao"].wait(ve, 16 * 2 * n_prev)
                ap = a % 2
                for oc in range(2):
                    ins = ve.tensor_copy(aob[:, oc * 512:(oc + 1) * 512],
                                         po[:, ap * 1024 + oc * 512:
                                             ap * 1024 + oc * 512 + 512])
                V.inc(ins, 1)
            v_ao[(tt, a)] = V.n
            v_ao_par[a % 2] = V.n

            @blk.gpsimd
            def _(gp):
                V.wait(gp, v_ao[(tt, a)])
                for oc in range(2):
                    C["stao"].inc(gp.dma_start(
                        out=ao_d[a * H + oc * 128:a * H + oc * 128 + 128,
                                 tt * 512:(tt + 1) * 512],
                        in_=aob[:, oc * 512:(oc + 1) * 512]), 16)

        for tt in range(TT):
            @blk.sync
            def _(sy, tt=tt):
                C["stln"].wait(sy, 16 * NT)
                if tt >= 1:
                    T.wait(sy, t_lastAT[tt - 1])
                for ec in range(8):
                    C["trln"].inc(sy.dma_start_transpose(
                        ln1T[:, ec * 512:(ec + 1) * 512],
                        ln1_d[tt * 512:(tt + 1) * 512, ec * 128:(ec + 1) * 128]), 16)
                for fc in range(tt * 8, tt * 8 + 4):
                    if fc >= 4:
                        V.wait(sy, v_w1q[fc - 4])
                    C["w1s%d" % (fc % 4)].inc(sy.dma_start(
                        out=w1r[:, (fc % 4) * D:(fc % 4 + 1) * D],
                        in_=w1_in[fc * 128:(fc + 1) * 128, :]), 16)

            for a in range(NA):
                for q in range(NA):
                    emit_AT(tt, a, q)
                if a >= 1:
                    emit_OP(tt, a - 1)
            emit_OP(tt, NA - 1)

            @blk.scalar
            def _(sc, tt=tt):
                C["stao"].wait(sc, 16 * 8 * (tt + 1))
                if tt > 0:
                    V.wait(sc, v_ln2ap[tt * 4 - 1])
                    A.wait(sc, A.n)
                for tc in range(4):
                    ti = tt * 4 + tc
                    C["trao"].inc(sc.dma_start_transpose(
                        aotm[:, tc * D:(tc + 1) * D],
                        ao_d[0:D, ti * 128:(ti + 1) * 128]), 16)

            @blk.vector
            def _(ve, tt=tt):
                C["trao"].wait(ve, 16 * 4 * (tt + 1))
                for tc in range(4):
                    ti = tt * 4 + tc
                    ins = ve.reduce_sum(ssum[:, 16 + ti:17 + ti],
                                        aotm[:, tc * D:(tc + 1) * D], axis=AX.X)
                V.inc(ins, 1)
            v_ln2sum = V.n

            @blk.vector
            def _(ve, tt=tt):
                C["trao"].wait(ve, 16 * 4 * (tt + 1))
                for tc in range(4):
                    ti = tt * 4 + tc
                    src_ = aotm[:, tc * D:(tc + 1) * D]
                    ve.tensor_tensor(out=sqs[:, 0:1024], in0=src_, in1=src_,
                                     op=OP.mult)
                    ve.reduce_sum(ssum[:, 16 + ti:17 + ti], src_, axis=AX.X)
                    ins = ve.reduce_sum(ssq[:, 16 + ti:17 + ti], sqs[:, 0:1024],
                                        axis=AX.X)
                V.inc(ins, 1)
            v_sq2 = [V.n]

            v_ln2fin = ln_finalize(16 + tt * 4, 4)

            for tc in range(4):
                ti = tt * 4 + tc

                @blk.vector
                def _(ve, tt=tt, tc=tc, ti=ti):
                    if tc == 0:
                        V.wait(ve, v_ln2fin)
                        ve.drain()
                    xt_ = xres[:, ti * D:(ti + 1) * D]
                    ve.tensor_scalar(out=scr[:, 1024:2048],
                                     in0=aotm[:, tc * D:(tc + 1) * D],
                                     scalar1=stm[:, 16 + ti:17 + ti],
                                     scalar2=srr[:, 16 + ti:17 + ti],
                                     op0=OP.subtract, op1=OP.mult)
                    ve.tensor_tensor(out=xt_, in0=xt_, in1=scr[:, 1024:2048],
                                     op=OP.add)
                    ve.tensor_tensor(out=sqs[:, 0:1024], in0=xt_, in1=xt_,
                                     op=OP.mult)
                    ve.tensor_copy(ssum[:, 32 + ti:33 + ti], ssum[:, ti:ti + 1])
                    ve.reduce_max(rx[:, ti:ti + 1], xt_, axis=AX.X)
                    V.inc(ve.reduce_sum(ssq[:, 32 + ti:33 + ti], sqs[:, 0:1024],
                                        axis=AX.X), 1)
                v_ln2ap[ti] = V.n

                @blk.gpsimd
                def _(gp, ti=ti):
                    V.wait(gp, v_ln2ap[ti])
                    C["stx1"].inc(gp.dma_start(out=x1_d[ti * 128:(ti + 1) * 128, :],
                                               in_=xres[:, ti * D:(ti + 1) * D]), 16)

            def emit_w1q(fc0, n):
                @blk.vector
                def _(ve):
                    if fc0 == 0:
                        T.wait(ve, t_btr)
                    for i, fc in enumerate(range(fc0, fc0 + n)):
                        C["w1s%d" % (fc % 4)].wait(ve, 16 * (fc // 4 + 1))
                        ve.reduce_max(tny[:, i:i + 1],
                                      w1r[:, (fc % 4) * D:(fc % 4 + 1) * D],
                                      axis=AX.X, apply_absolute_value=True)
                    ve.drain()
                    ve.tensor_scalar(out=sw1m[:, fc0:fc0 + n], in0=tny[:, 0:n],
                                     scalar1=EPS_Q, scalar2=None, op0=OP.max)
                    ve.drain()
                    ve.reciprocal(tny[:, 4:4 + n], sw1m[:, fc0:fc0 + n])
                    ve.drain()
                    ve.tensor_scalar(out=tny[:, 0:n], in0=tny[:, 4:4 + n],
                                     scalar1=127.0, scalar2=None, op0=OP.mult)
                    ve.drain()
                    for i, fc in enumerate(range(fc0, fc0 + n)):
                        if fc >= 2:
                            C["stj1"].wait(ve, 16 * fc)
                        ve.tensor_scalar(out=sci[:, (i % 2) * 1024:(i % 2 + 1) * 1024],
                                         in0=w1r[:, (fc % 4) * D:(fc % 4 + 1) * D],
                                         scalar1=tny[:, i:i + 1], scalar2=None,
                                         op0=OP.mult)
                        V.inc(ve.tensor_copy(j1r[:, (fc % 2) * D:(fc % 2 + 1) * D],
                                             sci[:, (i % 2) * 1024:(i % 2 + 1) * 1024]), 1)
                        v_w1q[fc] = V.n

                @blk.gpsimd
                def _(gp):
                    for fc in range(fc0, fc0 + n):
                        V.wait(gp, v_w1q[fc])
                        C["stj1"].inc(gp.dma_start(
                            out=j1_d[fc * 128:(fc + 1) * 128, :],
                            in_=j1r[:, (fc % 2) * D:(fc % 2 + 1) * D]), 16)

            emit_w1q(tt * 8, 4)

            @blk.sync
            def _(sy, tt=tt):
                for fc in range(tt * 8 + 4, tt * 8 + 8):
                    V.wait(sy, v_w1q[fc - 4])
                    C["w1s%d" % (fc % 4)].inc(sy.dma_start(
                        out=w1r[:, (fc % 4) * D:(fc % 4 + 1) * D],
                        in_=w1_in[fc * 128:(fc + 1) * 128, :]), 16)

            emit_w1q(tt * 8 + 4, 4)

        a_ln3sq = A.n

        @blk.vector
        def _(ve):
            A.wait(ve, a_ln3sq)
        v_ln3fin = ln_finalize(32, 16)

        attn.close()
        p2.close()
        ph_x.close()

        phF = ExitStack()
        pbc = phF.enter_context(nc.psum_tensor("pbc", [128, 16], F32))
        pcc = phF.enter_context(nc.psum_tensor("pcc", [1, 128], F32))
        x1rd = phF.enter_context(nc.sbuf_tensor("x1rd", [128, 2 * D], F32))

        @blk.vector
        def _(ve):
            V.wait(ve, v_ln3fin)
            ve.drain()
            ve.tensor_tensor(out=sA[:, 0:16], in0=rx[:, 0:16], in1=stm[:, 32:48],
                             op=OP.subtract)
            ve.drain()
            ve.tensor_tensor(out=sB[:, 0:16], in0=sA[:, 0:16], in1=srr[:, 32:48],
                             op=OP.mult)
            ve.drain()
            V.inc(ve.reduce_max(tny[:, 3:4], sB[:, 0:16], axis=AX.X), 1)
        v_lmax1 = V.n

        @blk.tensor
        def _(te):
            V.wait(te, v_lmax1)
            T.inc(te.transpose(pcc[0:1, :], tny[:, 3:4], ident[:]), 1)
        t_max1 = T.n

        @blk.vector
        def _(ve):
            T.wait(ve, t_max1)
            ve.reduce_max(gsc[0:1, 1:2], pcc[0:1, :], axis=AX.X)
            ve.drain()
            V.inc(ve.tensor_scalar(out=gsc[0:1, 0:1], in0=gsc[0:1, 1:2],
                                   scalar1=EPS_Q, scalar2=None, op0=OP.max), 1)
        v_g1 = V.n

        @blk.gpsimd
        def _(gp):
            V.wait(gp, v_g1)
            C["stx1"].wait(gp, 16 * NT)
            C["stj1"].wait(gp, 16 * 32)
            C["ccd"].inc(gp.dma_start(out=cc1_i[:], in_=gsc[:]), 16)
            C["ccd"].wait(gp)
            C["cc"].inc(gp.collective_compute(
                "AllReduce", OP.max, replica_groups=RG,
                ins=[cc1_i[:]], outs=[cc1_o[:]]), 1)
            C["cc"].wait(gp)
            C["ccd"].inc(gp.dma_start(out=gsc[0:1, 2:3], in_=cc1_o[0:1, 0:1]), 16)
            G.inc(gp.nop(), 1)
        g_cc1 = G.n

        @blk.tensor
        def _(te):
            C["ccd"].wait(te)
            G.wait(te, g_cc1)
            T.inc(te.matmul(pbc[:, 0:1], ones[:], gsc[0:1, 2:3],
                            start=True, stop=True), 1)
        t_bc1 = T.n

        @blk.vector
        def _(ve):
            T.wait(ve, t_bc1)
            V.wait(ve, v_bc)
            ve.tensor_copy(gb[:, 0:1], pbc[:, 0:1])
            ve.drain()
            ve.reciprocal(gb[:, 1:2], gb[:, 0:1])
            ve.drain()
            ve.tensor_scalar(out=gb[:, 2:3], in0=gb[:, 1:2], scalar1=255.0,
                             scalar2=None, op0=OP.mult)
            ve.tensor_scalar(out=gb[:, 3:4], in0=gb[:, 0:1], scalar1=1.0 / 255.0,
                             scalar2=None, op0=OP.mult)
            ve.drain()
            ve.tensor_scalar(out=sb1[:], in0=sw1m[:], scalar1=gb[:, 3:4],
                             scalar2=1.0 / 127.0, op0=OP.mult, op1=OP.mult)
            ve.drain()
            ve.reciprocal(sA[:, 0:32], sb1[:])
            ve.drain()
            ve.tensor_tensor(out=sB[:, 0:32], in0=b1c[:], in1=sA[:, 0:32],
                             op=OP.mult)
            ve.drain()
            ve.tensor_copy(sci[:, 0:32], sB[:, 0:32])
            ve.drain()
            ve.tensor_copy(bq1[:], sci[:, 0:32])
            ve.drain()
            ve.tensor_tensor(out=eb1[:], in0=bq1[:], in1=sb1[:], op=OP.mult)
            ve.tensor_scalar(out=sB[:, 0:16], in0=srr[:, 32:48], scalar1=gb[:, 2:3],
                             scalar2=None, op0=OP.mult)
            ve.drain()
            ve.tensor_tensor(out=sDs[:, 0:16], in0=stm[:, 32:48], in1=sB[:, 0:16],
                             op=OP.mult)
            ve.drain()
            V.inc(ve.tensor_scalar(out=sC[:, 0:16], in0=sDs[:, 0:16], scalar1=-1.0,
                                   scalar2=None, op0=OP.mult), 1)
        v_p4 = V.n

        phK = ExitStack()
        k1T = phK.enter_context(nc.sbuf_tensor("k1T", [128, 8 * S], BF16))
        k1r = phK.enter_context(nc.sbuf_tensor("k1r", [128, 2 * D], F32))
        if DEBUG_OUTPUTS:
            k1rb = phK.enter_context(nc.sbuf_tensor("k1rb", [128, 2 * D], BF16))

        phP = ExitStack()
        pk = phP.enter_context(nc.psum_tensor("pk", [128, 2048], F32))

        a_k1v = []
        v_k1 = []
        t_k1t = []
        a_k1e = []
        v_k1e = []
        for ti in range(NT):
            @blk.sync
            def _(sy, ti=ti):
                C["stx1"].wait(sy, 16 * NT)
                if ti == 0:
                    C["cc"].wait(sy, 1)
                if ti >= 2:
                    A.wait(sy, a_k1v[ti - 2])
                C["x1p%d" % (ti % 2)].inc(
                    sy.dma_start(out=x1rd[:, (ti % 2) * D:(ti % 2 + 1) * D],
                                 in_=x1_d[ti * 128:(ti + 1) * 128, :]), 16)

            @blk.scalar
            def _(sc, ti=ti):
                if ti == 0:
                    V.wait(sc, v_p4)
                C["x1p%d" % (ti % 2)].wait(sc, 16 * (ti // 2 + 1))
                if ti >= 2:
                    V.wait(sc, v_k1[ti - 2])
                A.inc(sc.activation(out=scr[:, (ti % 2) * 1024:(ti % 2) * 1024 + 1024],
                                    in_=x1rd[:, (ti % 2) * D:(ti % 2 + 1) * D],
                                    func=AF.Relu, bias=sC[:, ti:ti + 1],
                                    scale=sB[:, ti:ti + 1]), 1)
            a_k1v.append(A.n)

            @blk.vector
            def _(ve, ti=ti):
                A.wait(ve, a_k1v[ti])
                if ti >= 2:
                    T.wait(ve, t_k1t[ti - 2])
                    if DEBUG_OUTPUTS:
                        C["stk1"].wait(ve, 16 * ti)
                ve.tensor_copy(sci[:, (ti % 2) * 1024:(ti % 2) * 1024 + 1024],
                               scr[:, (ti % 2) * 1024:(ti % 2) * 1024 + 1024])
                V.inc(ve.tensor_copy(k1r[:, (ti % 2) * D:(ti % 2 + 1) * D],
                                     sci[:, (ti % 2) * 1024:(ti % 2) * 1024 + 1024]), 1)
            v_k1.append(V.n)

            if DEBUG_OUTPUTS:
                @blk.vector
                def _(ve, ti=ti):
                    V.inc(ve.tensor_copy(k1rb[:, (ti % 2) * D:(ti % 2 + 1) * D],
                                         k1r[:, (ti % 2) * D:(ti % 2 + 1) * D]), 1)
                v_k1[-1] = V.n

                @blk.gpsimd
                def _(gp, ti=ti):
                    V.wait(gp, v_k1[ti])
                    C["stk1"].inc(gp.dma_start(out=k1_d[ti * 128:(ti + 1) * 128, :],
                                               in_=k1rb[:, (ti % 2) * D:(ti % 2 + 1) * D]), 16)

            @blk.tensor
            def _(te, ti=ti):
                V.wait(te, v_k1[ti])
                if ti >= 2:
                    A.wait(te, a_k1e[ti - 2])
                    V.wait(te, v_k1e[ti - 2])
                for dc in range(8):
                    ins = te.matmul(
                        pk[:, (ti % 2) * 1024 + dc * 128:(ti % 2) * 1024 + dc * 128 + 128],
                        k1r[:, (ti % 2) * D + dc * 128:(ti % 2) * D + dc * 128 + 128],
                        ident[:],
                        start=(dc % 4 == 0), stop=(dc % 4 == 3),
                        is_transpose=True)
                T.inc(ins, 1)
            t_k1t.append(T.n)

            @blk.scalar
            def _(sc, ti=ti):
                T.wait(sc, t_k1t[ti])
                for dc in range(4):
                    ins = sc.activation(
                        out=k1T[:, dc * S + ti * 128:dc * S + ti * 128 + 128],
                        in_=pk[:, (ti % 2) * 1024 + dc * 128:(ti % 2) * 1024 + dc * 128 + 128],
                        func=AF.Copy, bias=0.0, scale=1.0)
                A.inc(ins, 1)
            a_k1e.append(A.n)

            @blk.vector
            def _(ve, ti=ti):
                T.wait(ve, t_k1t[ti])
                for dc in range(4, 8):
                    ins = ve.tensor_copy(
                        k1T[:, dc * S + ti * 128:dc * S + ti * 128 + 128],
                        pk[:, (ti % 2) * 1024 + dc * 128:(ti % 2) * 1024 + dc * 128 + 128])
                V.inc(ins, 1)
            v_k1e.append(V.n)

        phP.close()

        ph6 = ExitStack()
        j1T = ph6.enter_context(nc.sbuf_tensor("j1T", [128, 8 * FF], BF16))
        w2r = ph6.enter_context(nc.sbuf_tensor("w2r", [128, 2 * FF], F32))
        j2r = ph6.enter_context(nc.sbuf_tensor("j2r", [128, 2 * FF], BF16))
        hr = ph6.enter_context(nc.sbuf_tensor("hr", [128, 6 * 512], F16))
        p6 = ph6.enter_context(nc.psum_tensor("p6", [128, 3072], F32))

        @blk.sync
        def _(sy):
            C["stj1"].wait(sy, 16 * 32)
            C["cc"].wait(sy, 1)
            for dc in range(8):
                C["trj1"].inc(sy.dma_start_transpose(
                    j1T[:, dc * FF:(dc + 1) * FF],
                    j1_d[0:FF, dc * 128:(dc + 1) * 128]), 16)

        v_w2q = [0] * 8
        t_h = []
        b_free = []
        v_h = []
        v_hm = []
        idx = 0
        for fc in range(32):
            for tt in range(TT):
                bank = idx % 6

                @blk.tensor
                def _(te, fc=fc, tt=tt, bank=bank, idx=idx):
                    if idx == 0:
                        A.wait(te, a_k1e[-1])
                        V.wait(te, v_k1e[-1])
                        C["trj1"].wait(te, 16 * 8)
                    if idx >= 6:
                        V.wait(te, b_free[idx - 6])
                    for dc in range(8):
                        ins = te.matmul(
                            p6[:, bank * 512:(bank + 1) * 512],
                            j1T[:, dc * FF + fc * 128:dc * FF + fc * 128 + 128],
                            k1T[:, dc * S + tt * 512:dc * S + tt * 512 + 512],
                            start=(dc == 0), stop=(dc == 7))
                    T.inc(ins, 1)
                t_h.append(T.n)

                @blk.vector
                def _(ve, fc=fc, bank=bank, idx=idx):
                    T.wait(ve, t_h[idx])
                    if idx >= 6:
                        C["sth"].wait(ve, 16 * idx)
                    V.inc(ve.tensor_scalar(out=scr[:, 0:512],
                                           in0=p6[:, bank * 512:(bank + 1) * 512],
                                           scalar1=sb1[:, fc:fc + 1],
                                           scalar2=eb1[:, fc:fc + 1],
                                           op0=OP.mult, op1=OP.add), 1)
                    b_free.append(V.n)
                    V.inc(ve.tensor_scalar(out=hr[:, (idx % 6) * 512:(idx % 6 + 1) * 512],
                                           in0=scr[:, 0:512], scalar1=0.0,
                                           scalar2=None, op0=OP.max), 1)
                    v_h.append(V.n)
                    V.inc(ve.reduce_max(hm[:, idx:idx + 1],
                                        hr[:, (idx % 6) * 512:(idx % 6 + 1) * 512],
                                        axis=AX.X), 1)
                    v_hm.append(V.n)

                @blk.sync
                def _(sy, fc=fc, tt=tt, idx=idx):
                    V.wait(sy, v_h[idx])
                    C["sth"].inc(sy.dma_start(
                        out=h_d[fc * 128:(fc + 1) * 128, tt * 512:(tt + 1) * 512],
                        in_=hr[:, (idx % 6) * 512:(idx % 6 + 1) * 512]), 16)
                idx += 1

        for oc in range(8):

            @blk.sync
            def _(sy, oc=oc):
                if oc >= 2:
                    V.wait(sy, v_w2q[oc - 2])
                C["w2s%d" % (oc % 2)].inc(sy.dma_start(
                    out=w2r[:, (oc % 2) * FF:(oc % 2 + 1) * FF],
                    in_=w2_in[oc * 128:(oc + 1) * 128, :]), 16)

            @blk.vector
            def _(ve, oc=oc):
                C["w2s%d" % (oc % 2)].wait(ve, 16 * (oc // 2 + 1))
                if oc >= 2:
                    C["stj2"].wait(ve, 16 * oc)
                wsl = w2r[:, (oc % 2) * FF:(oc % 2 + 1) * FF]
                ve.reduce_max(tny[:, 4:5], wsl, axis=AX.X,
                              apply_absolute_value=True)
                ve.drain()
                ve.tensor_scalar(out=sw2m[:, oc:oc + 1], in0=tny[:, 4:5],
                                 scalar1=EPS_Q, scalar2=None, op0=OP.max)
                ve.drain()
                ve.reciprocal(tny[:, 5:6], sw2m[:, oc:oc + 1])
                ve.drain()
                ve.tensor_scalar(out=tny[:, 6:7], in0=tny[:, 5:6],
                                 scalar1=127.0, scalar2=None, op0=OP.mult)
                ve.drain()
                for half in range(2):
                    ve.tensor_scalar(out=sci[:, (half % 2) * 2048:
                                         (half % 2) * 2048 + 2048][:, 0:2048]
                                     if False else sci[:, 0:2048],
                                     in0=wsl[:, half * 2048:(half + 1) * 2048],
                                     scalar1=tny[:, 6:7], scalar2=None,
                                     op0=OP.mult)
                    ins = ve.tensor_copy(
                        j2r[:, (oc % 2) * FF + half * 2048:
                            (oc % 2) * FF + half * 2048 + 2048],
                        sci[:, 0:2048])
                V.inc(ins, 1)
                v_w2q[oc] = V.n

            @blk.gpsimd
            def _(gp, oc=oc):
                V.wait(gp, v_w2q[oc])
                C["stj2"].inc(gp.dma_start(
                    out=j2_d[oc * 128:(oc + 1) * 128, :],
                    in_=j2r[:, (oc % 2) * FF:(oc % 2 + 1) * FF]), 16)

        ph6.close()
        phK.close()

        @blk.vector
        def _(ve):
            V.wait(ve, v_hm[-1])
            ve.drain()
            V.inc(ve.reduce_max(tny[:, 7:8], hm[:, 0:128], axis=AX.X), 1)
        v_lmax2 = V.n

        @blk.tensor
        def _(te):
            V.wait(te, v_lmax2)
            T.inc(te.transpose(pcc[0:1, :], tny[:, 7:8], ident[:]), 1)
        t_max2 = T.n

        @blk.vector
        def _(ve):
            T.wait(ve, t_max2)
            ve.reduce_max(gsc[0:1, 1:2], pcc[0:1, :], axis=AX.X)
            ve.drain()
            V.inc(ve.tensor_scalar(out=gsc[0:1, 0:1], in0=gsc[0:1, 1:2],
                                   scalar1=EPS_Q, scalar2=None, op0=OP.max), 1)
        v_g2 = V.n

        @blk.gpsimd
        def _(gp):
            V.wait(gp, v_g2)
            C["sth"].wait(gp, 16 * 128)
            C["stj2"].wait(gp, 16 * 8)
            C["ccd"].inc(gp.dma_start(out=cc2_i[:], in_=gsc[:]), 16)
            C["ccd"].wait(gp)
            C["cc"].inc(gp.collective_compute(
                "AllReduce", OP.max, replica_groups=RG,
                ins=[cc2_i[:]], outs=[cc2_o[:]]), 1)
            C["cc"].wait(gp)
            C["ccd"].inc(gp.dma_start(out=gsc[0:1, 2:3], in_=cc2_o[0:1, 0:1]), 16)
            G.inc(gp.nop(), 1)
        g_cc2 = G.n

        @blk.tensor
        def _(te):
            C["ccd"].wait(te)
            G.wait(te, g_cc2)
            T.inc(te.matmul(pbc[:, 1:2], ones[:], gsc[0:1, 2:3],
                            start=True, stop=True), 1)
        t_bc2 = T.n


        ph7 = ExitStack()
        j2T = ph7.enter_context(nc.sbuf_tensor("j2T", [128, 32 * 1024], BF16))
        k2b = ph7.enter_context(nc.sbuf_tensor("k2b", [128, 2 * 32 * 512], BF16))
        h2r = ph7.enter_context(nc.sbuf_tensor("h2r", [128, 2 * 2048], F16))
        o2tm = ph7.enter_context(nc.sbuf_tensor("o2tm", [128, 4 * D], BF16))
        x1r4 = ph7.enter_context(nc.sbuf_tensor("x1r4", [128, 4 * D], F32))
        outr = ph7.enter_context(nc.sbuf_tensor("outr", [128, 2 * D], F32))
        o2r = ph7.enter_context(nc.sbuf_tensor("o2r", [128, 2 * 512], BF16))
        p7 = ph7.enter_context(nc.psum_tensor("p7", [128, 2048], F32))

        @blk.vector
        def _(ve):
            T.wait(ve, t_bc2)
            ve.tensor_copy(gb[:, 4:5], pbc[:, 1:2])
            ve.drain()
            ve.reciprocal(gb[:, 5:6], gb[:, 4:5])
            ve.drain()
            ve.tensor_scalar(out=gb[:, 6:7], in0=gb[:, 5:6], scalar1=255.0,
                             scalar2=None, op0=OP.mult)
            V.inc(ve.tensor_scalar(out=gb[:, 7:8], in0=gb[:, 4:5],
                                   scalar1=1.0 / 255.0,
                                   scalar2=None, op0=OP.mult), 1)
        v_p7a = V.n

        @blk.vector
        def _(ve):
            V.wait(ve, v_p7a)
            ve.drain()
            ve.tensor_scalar(out=sb2[:], in0=sw2m[:], scalar1=gb[:, 7:8],
                             scalar2=1.0 / 127.0, op0=OP.mult, op1=OP.mult)
            ve.drain()
            ve.reciprocal(sA[:, 0:8], sb2[:])
            ve.drain()
            ve.tensor_tensor(out=sB[:, 0:8], in0=b2c[:], in1=sA[:, 0:8], op=OP.mult)
            ve.drain()
            ve.tensor_copy(sci[:, 0:8], sB[:, 0:8])
            ve.drain()
            ve.tensor_copy(bq2[:], sci[:, 0:8])
            ve.drain()
            V.inc(ve.tensor_tensor(out=eb2[:], in0=bq2[:], in1=sb2[:], op=OP.mult), 1)
        v_p7 = V.n


        @blk.sync
        def _(sy):
            C["stj2"].wait(sy, 16 * 8)
            C["cc"].wait(sy, 2)
            for fc in range(32):
                C["trj2"].inc(sy.dma_start_transpose(
                    j2T[:, fc * 1024:(fc + 1) * 1024],
                    j2_d[0:D, fc * 128:(fc + 1) * 128]), 16)

        def emit_P8(tt):
                @blk.scalar
                def _(sc, tt=tt):
                    C["sto2"].wait(sc, 16 * 8 * (tt + 1))
                    if tt > 0:
                        V.wait(sc, v_outc[-1])
                        A.wait(sc, A.n)
                    for tc in range(4):
                        ti = tt * 4 + tc
                        C["tro2"].inc(sc.dma_start_transpose(
                            o2tm[:, tc * D:(tc + 1) * D],
                            o2_d[0:D, ti * 128:(ti + 1) * 128]), 16)
                        C["x1p%d" % (ti % 2)].inc(sc.dma_start(
                            out=x1r4[:, tc * D:(tc + 1) * D],
                            in_=x1_d[ti * 128:(ti + 1) * 128, :]), 16)

                @blk.vector
                def _(ve, tt=tt):
                    C["tro2"].wait(ve, 16 * 4 * (tt + 1))
                    for tc in range(4):
                        ins = ve.reduce_sum(ssum[:, 48 + tt * 4 + tc:49 + tt * 4 + tc],
                                            o2tm[:, tc * D:(tc + 1) * D], axis=AX.X)
                    V.inc(ins, 1)
                v_ln4sum = V.n

                @blk.vector
                def _(ve, tt=tt):
                    C["tro2"].wait(ve, 16 * 4 * (tt + 1))
                    for tc in range(4):
                        ti = tt * 4 + tc
                        src_ = o2tm[:, tc * D:(tc + 1) * D]
                        ve.tensor_tensor(out=scr[:, 0:1024], in0=src_, in1=src_,
                                         op=OP.mult)
                        ve.reduce_sum(ssum[:, 48 + ti:49 + ti], src_, axis=AX.X)
                        ins = ve.reduce_sum(ssq[:, 48 + ti:49 + ti], scr[:, 0:1024],
                                            axis=AX.X)
                    V.inc(ins, 1)

                v_ln4fin = ln_finalize(48 + tt * 4, 4)

                for tc in range(4):
                    ti = tt * 4 + tc

                    @blk.vector
                    def _(ve, tt=tt, tc=tc, ti=ti):
                        if tc == 0:
                            V.wait(ve, v_ln4fin)
                            ve.drain()
                        C["x1p%d" % (ti % 2)].wait(ve, 16 * (8 + ti // 2 + 1))
                        if len(v_outc) >= 2:
                            C["stout"].wait(ve, 16 * len(v_outc))
                        ve.tensor_scalar(out=scr[:, 1024:2048],
                                         in0=o2tm[:, tc * D:(tc + 1) * D],
                                         scalar1=stm[:, 48 + ti:49 + ti],
                                         scalar2=srr[:, 48 + ti:49 + ti],
                                         op0=OP.subtract, op1=OP.mult)
                        V.inc(ve.tensor_tensor(out=outr[:, (tc % 2) * D:(tc % 2 + 1) * D],
                                               in0=x1r4[:, tc * D:(tc + 1) * D],
                                               in1=scr[:, 1024:2048], op=OP.add), 1)
                    v_outc.append(V.n)

                    @blk.gpsimd
                    def _(gp, ti=ti, tc=tc):
                        V.wait(gp, v_outc[-1])
                        C["stout"].inc(gp.dma_start(
                            out=y_out[ti * 128:(ti + 1) * 128, :],
                            in_=outr[:, (tc % 2) * D:(tc % 2 + 1) * D]), 16)



        t_o2 = {}
        a_o2 = {}
        v_k2g = {}
        v_outc = []
        for tt in range(TT):
            for g in range(8):
                gi = tt * 8 + g

                @blk.gpsimd
                def _(gp, tt=tt, g=g, gi=gi):
                    C["sth"].wait(gp, 16 * 128)
                    if gi == 0:
                        C["cc"].wait(gp, 2)
                    if gi >= 2:
                        V.wait(gp, v_k2g[gi - 2])
                    C["ldhp%d" % (gi % 2)].inc(gp.dma_start(
                        out=h2r[:, (gi % 2) * 2048:(gi % 2 + 1) * 2048],
                        in_=h_d[g * 512:(g + 1) * 512,
                                tt * 512:(tt + 1) * 512].rearrange(
                                    "(c p) t -> p c t", p=128)), 16)

                @blk.vector
                def _(ve, tt=tt, g=g, gi=gi):
                    if gi == 0:
                        V.wait(ve, v_p7)
                    C["ldhp%d" % (gi % 2)].wait(ve, 16 * (gi // 2 + 1))
                    if tt >= 2 and g == 0:
                        T.wait(ve, t_o2[(tt - 2, 7)])
                    ve.tensor_scalar(out=sci[:, 0:2048],
                                     in0=h2r[:, (gi % 2) * 2048:(gi % 2 + 1) * 2048],
                                     scalar1=gb[:, 6:7], scalar2=None, op0=OP.mult)
                    ve.drain()
                    ins = ve.tensor_copy(
                        k2b[:, (tt % 2) * 16384 + g * 2048:
                            (tt % 2) * 16384 + (g + 1) * 2048],
                        sci[:, 0:2048])
                    ve.drain()
                    V.inc(ins, 1)
                v_k2g[gi] = V.n

            for do in range(8):
                o2i = tt * 8 + do
                bank = o2i % 4

                @blk.tensor
                def _(te, tt=tt, do=do, bank=bank, o2i=o2i):
                    if tt == 0 and do == 0:
                        V.wait(te, b_free[-1])
                    if do == 0:
                        V.wait(te, v_k2g[tt * 8 + 7])
                        C["trj2"].wait(te, 16 * 32)
                    if o2i >= 4:
                        V.wait(te, a_o2[o2i - 4])
                    for fc in range(32):
                        ins = te.matmul(
                            p7[:, bank * 512:(bank + 1) * 512],
                            j2T[:, fc * 1024 + do * 128:fc * 1024 + do * 128 + 128],
                            k2b[:, (tt % 2) * 16384 + fc * 512:
                                (tt % 2) * 16384 + fc * 512 + 512],
                            start=(fc == 0), stop=(fc == 31))
                    T.inc(ins, 1)
                t_o2[(tt, do)] = T.n

                @blk.vector
                def _(ve, tt=tt, do=do, bank=bank, o2i=o2i):
                    T.wait(ve, t_o2[(tt, do)])
                    if o2i >= 2:
                        C["sto2"].wait(ve, 16 * o2i)
                    V.inc(ve.tensor_scalar(
                        out=o2r[:, (do % 2) * 512:(do % 2 + 1) * 512],
                        in0=p7[:, bank * 512:(bank + 1) * 512],
                        scalar1=bq2[:, do:do + 1], scalar2=sb2[:, do:do + 1],
                        op0=OP.add, op1=OP.mult), 1)
                a_o2[o2i] = V.n

                @blk.sync
                def _(sy, tt=tt, do=do):
                    V.wait(sy, a_o2[tt * 8 + do])
                    C["sto2"].inc(sy.dma_start(
                        out=o2_d[do * 128:(do + 1) * 128, tt * 512:(tt + 1) * 512],
                        in_=o2r[:, (do % 2) * 512:(do % 2 + 1) * 512]), 16)

            if tt >= 1:
                emit_P8(tt - 1)

        emit_P8(TT - 1)

        @blk.gpsimd
        def _(gp):
            C["stout"].wait(gp, 16 * NT)
            C["stout"].inc(gp.dma_start(out=dbg_d[:, 0:64], in_=ssum[:]), 16)
            C["stout"].inc(gp.dma_start(out=dbg_d[:, 64:128], in_=ssq[:]), 16)
            C["stout"].inc(gp.dma_start(out=dbg_d[:, 128:192], in_=stm[:]), 16)
            C["stout"].inc(gp.dma_start(out=dbg_d[:, 192:256], in_=srr[:]), 16)
            if DEBUG_OUTPUTS:
                C["stout"].inc(gp.dma_start(out=dbg2_d[:, 192:200], in_=sw2m[:]), 16)
                C["stout"].inc(gp.dma_start(out=dbg2_d[:, 200:208], in_=sb2[:]), 16)
                C["stout"].inc(gp.dma_start(out=dbg2_d[:, 208:216], in_=gb[:]), 16)
                C["stout"].wait(gp, 16 * NT + 64 + 48)
            C["stout"].wait(gp, 16 * NT + 64)

        ph7.close()
        phF.close()

        for name, ctr in C.items():
            assert ctr.max_waited <= ctr.n, (name, ctr.max_waited, ctr.n)

    return nc


_NC_CACHE = {}


def kernel(L0_attn_hidden_states, L0_attn_queries, L0_attn_combiners,
           L0_ffn_w1, L0_ffn_b1, L0_ffn_w2, L0_ffn_b2):
    x = np.ascontiguousarray(np.asarray(L0_attn_hidden_states, dtype=np.float32))
    qw = np.ascontiguousarray(np.asarray(L0_attn_queries, dtype=np.float32))
    cb = np.ascontiguousarray(np.asarray(L0_attn_combiners, dtype=np.float32))
    w1 = np.ascontiguousarray(np.asarray(L0_ffn_w1, dtype=np.float32))
    b1 = np.ascontiguousarray(np.asarray(L0_ffn_b1, dtype=np.float32))
    w2 = np.ascontiguousarray(np.asarray(L0_ffn_w2, dtype=np.float32))
    b2 = np.ascontiguousarray(np.asarray(L0_ffn_b2, dtype=np.float32))

    if "nc" not in _NC_CACHE:
        _NC_CACHE["nc"] = build_nc()
    nc = _NC_CACHE["nc"]

    qwT = np.ascontiguousarray(qw.transpose(0, 2, 1))
    in_maps = [{"x0": x[b], "qwT": qwT, "cb": cb, "w1": w1, "b1": b1,
                "w2": w2, "b2": b2} for b in range(8)]
    res = run_bass_kernel_spmd(nc, in_maps, core_ids=list(range(8)))
    return np.stack([np.asarray(r["y"], dtype=np.float32) for r in res.results])
